# version 07
# speedup vs baseline: 1.3274x; 1.0170x over previous
import numpy as np
import ml_dtypes
from contextlib import ExitStack
import concourse.bass as bass
import concourse.mybir as mybir
from concourse.bass_utils import run_bass_kernel_spmd

F32 = mybir.dt.float32
BF16 = mybir.dt.bfloat16
AF = mybir.ActivationFunctionType
ALU = mybir.AluOpType
AX = mybir.AxisListType
BF = ml_dtypes.bfloat16

D = 1024
N = 2048
NT = N // 128
DC = D // 128
NBIG = 80.0
BAND_OFF = 2048
STEP_OFF = 2432
ENGINES = ("tensor", "vector", "scalar", "gpsimd", "sync")
ALL_STEPS = ("l0a", "moe", "l0b", "moe", "l1a", "smoe", "l1b", "smoe")


def _f(a):
    return np.ascontiguousarray(np.asarray(a), dtype=np.float32)


def prep_weights(I):
    W = {}
    wqkv = np.asarray(I["L0_attn_a_W_qkv"])
    wr = wqkv.reshape(16, 3, 64, 1024)
    W["l0a_wqkvT"] = _f(np.concatenate([wr[:, 0].reshape(1024, 1024),
                                        wr[:, 1].reshape(1024, 1024),
                                        wr[:, 2].reshape(1024, 1024)], 0).T)
    W["l0a_woutT"] = _f(np.asarray(I["L0_attn_a_W_out"]).T)
    W["l0a_bout_row"] = _f(np.asarray(I["L0_attn_a_b_out"]).reshape(1, 1024))

    w12 = np.asarray(I["L0_ffn_w12"])
    w3 = np.asarray(I["L0_ffn_w3"])
    sw1 = np.asarray(I["L0_ffn_shared_w1"])
    sw2 = np.asarray(I["L0_ffn_shared_w2"])
    sw3 = np.asarray(I["L0_ffn_shared_w3"])
    w12T = [np.ascontiguousarray(w12[e].T) for e in range(8)]
    w12T += [np.ascontiguousarray(np.concatenate(
        [sw1[1024 * u:1024 * (u + 1)], sw2[1024 * u:1024 * (u + 1)]], 0).T)
        for u in range(4)]
    w12Ti = [w.reshape(1024, 2, 8, 128).transpose(0, 2, 1, 3).reshape(1024, 2048)
             for w in w12T]
    W["moe_w12T"] = _f(np.stack(w12Ti))
    w3T = [np.ascontiguousarray(w3[e].T) for e in range(8)]
    w3T += [np.ascontiguousarray(sw3[:, 1024 * u:1024 * (u + 1)].T) for u in range(4)]
    W["moe_w3T"] = _f(np.stack(w3T))
    W["moe_routerT"] = _f(np.asarray(I["L0_ffn_router_w"]).T)
    W["moe_ebias_row"] = _f(np.asarray(I["L0_ffn_expert_bias"]).reshape(1, 8))

    W["l0b_wqT"] = _f(np.asarray(I["L0_attn_b_W_q"]).T)
    W["l0b_wkT"] = _f(np.asarray(I["L0_attn_b_W_k"]).T)
    W["l0b_wvT"] = _f(np.asarray(I["L0_attn_b_W_v"]).T)
    W["l0b_woT"] = _f(np.asarray(I["L0_attn_b_W_o"]).T)
    kl1 = np.asarray(I["L0_attn_b_k_lod1"])
    kgt = kl1.reshape(64, 32, 64).transpose(2, 1, 0).reshape(64, 2048)
    W["l0b_klod1gT2"] = _f(np.concatenate([kgt, kgt], 0))

    W["l1a_wdq"] = _f(I["L1_attn_a_W_DQ"])
    W["l1a_bdq_col"] = _f(np.asarray(I["L1_attn_a_b_DQ"]).reshape(32, 1))
    W["l1a_wdkv"] = _f(I["L1_attn_a_W_DKV"])
    W["l1a_bdkv_col"] = _f(np.asarray(I["L1_attn_a_b_DKV"]).reshape(32, 1))
    W["l1a_wuq"] = _f(I["L1_attn_a_W_UQ"])
    W["l1a_buq_col"] = _f(np.asarray(I["L1_attn_a_b_UQ"]).reshape(8, 128).T)
    W["l1a_wuk"] = _f(I["L1_attn_a_W_UK"])
    W["l1a_buk_col"] = _f(np.asarray(I["L1_attn_a_b_UK"]).reshape(8, 128).T)
    W["l1a_wuv_aug"] = _f(np.concatenate(
        [np.asarray(I["L1_attn_a_W_UV"]),
         np.asarray(I["L1_attn_a_b_UV"]).reshape(1, 1024)], 0))
    W["l1a_wqr"] = _f(I["L1_attn_a_W_QR"])
    W["l1a_bqr_col"] = _f(np.asarray(I["L1_attn_a_b_QR"]).reshape(4, 64).T)
    W["l1a_wkr"] = _f(I["L1_attn_a_W_KR"])
    W["l1a_bkr_col"] = _f(np.asarray(I["L1_attn_a_b_KR"]).reshape(32, 1))
    W["l1a_wo"] = _f(I["L1_attn_a_W_O"])
    W["l1a_bo_row"] = _f(np.asarray(I["L1_attn_a_b_O"]).reshape(1, 1024))

    W["l1b_wqT"] = _f(np.asarray(I["L1_attn_b_Wq_w"]).T)
    W["l1b_bq_col"] = _f(np.asarray(I["L1_attn_b_Wq_b"]).reshape(8, 128).T)
    W["l1b_wkT"] = _f(np.asarray(I["L1_attn_b_Wk_w"]).T)
    W["l1b_bk_col"] = _f(np.asarray(I["L1_attn_b_Wk_b"]).reshape(8, 128).T)
    W["l1b_wvT"] = _f(np.asarray(I["L1_attn_b_Wv_w"]).T)
    W["l1b_bv_row"] = _f(np.asarray(I["L1_attn_b_Wv_b"]).reshape(1, 1024))
    W["l1b_woT"] = _f(np.asarray(I["L1_attn_b_Wo_w"]).T)
    W["l1b_bo_row"] = _f(np.asarray(I["L1_attn_b_Wo_b"]).reshape(1, 1024))

    W["sm_mu"] = _f(np.asarray(I["L1_ffn_mu"]).reshape(1024, 256))
    W["sm_gamma_col"] = _f(np.asarray(I["L1_ffn_ln_gamma"]).reshape(8, 128).T)
    W["sm_beta_col"] = _f(np.asarray(I["L1_ffn_ln_beta"]).reshape(8, 128).T)
    W["sm_scale"] = _f(np.asarray(I["L1_ffn_scale"]).reshape(1, 1))
    W["sm_w1"] = _f(I["L1_ffn_w1"])
    W["sm_b1"] = _f(np.asarray(I["L1_ffn_b1"]).reshape(16, 1, 256))
    W["sm_w2"] = _f(I["L1_ffn_w2"])
    W["sm_b2"] = _f(np.asarray(I["L1_ffn_b2"]).reshape(16, 1, 1024))

    W["ident_bf"] = np.eye(128, dtype=BF)
    sel8 = np.zeros((8, 1024), np.float32)
    for e in range(8):
        sel8[e, 128 * e:128 * (e + 1)] = 1.0
    W["sel8"] = sel8.astype(BF)
    W["ones_col_bf"] = np.ones((128, 1), dtype=BF)
    W["ones_row"] = np.ones((1, 2048), np.float32)
    W["ones_row_bf"] = np.ones((1, 128), dtype=BF)

    inv_freq = 1.0 / (10000.0 ** (np.arange(0, 64, 2, dtype=np.float64) / 64.0))
    freqs = np.arange(2048, dtype=np.float64)[:, None] * inv_freq[None, :]
    emb = np.concatenate([freqs, freqs], -1)
    cosT = np.cos(emb).T.astype(np.float32)
    sinT = np.sin(emb).T.astype(np.float32)
    W["rope_cos2"] = _f(np.concatenate([cosT, cosT], 0))
    W["rope_sin2"] = _f(np.concatenate([sinT, sinT], 0))
    pi64 = np.zeros((64, 64), np.float32)
    for i in range(32):
        pi64[i, 32 + i] = -1.0
        pi64[32 + i, i] = 1.0
    pi128 = np.zeros((128, 128), np.float32)
    pi128[:64, :64] = pi64
    pi128[64:, 64:] = pi64
    W["pi128T"] = pi128.T.astype(BF)
    t_idx = np.arange(64)[:, None]
    k_idx = np.arange(2048)[None, :]
    eb = (NBIG * (k_idx // 32 == t_idx)).astype(np.float32)
    W["ebig2"] = np.concatenate([eb, eb], 0).astype(BF)
    r = np.arange(128)[:, None]
    u = np.arange(4480)[None, :]
    dqk = u - BAND_OFF - r
    W["mask_band"] = ((dqk >= 0) & (dqk <= 511)).astype(BF)
    u2 = np.arange(4992)[None, :]
    W["mask_step"] = ((u2 - STEP_OFF - r) >= 0).astype(BF)
    return W


class P:
    def __init__(self, nc, es):
        self.nc = nc
        self.es = es
        self.q = {e: [] for e in ENGINES}
        self.sems = {}
        self.cnt = {}

    def sem(self, name):
        if name not in self.sems:
            self.sems[name] = self.es.enter_context(self.nc.semaphore(name))
            self.cnt[name] = 0
        return self.sems[name]

    CHAIN = {"vector": "dve", "scalar": "act", "gpsimd": "gp"}

    def _emit(self, eng, fn, inc, wait):
        w = []
        ch = self.CHAIN.get(eng)
        if ch is not None and fn is not None and inc == ch:
            prev = self.cnt.get(ch, 0)
            if prev > 0:
                self.sem(ch)
                w.append((ch, prev))
        for it in (wait or []):
            if it is None:
                continue
            s, cc = it
            self.sem(s)
            w.append((s, cc))
        newc = None
        if inc:
            self.sem(inc)
            self.cnt[inc] += 16 if inc.startswith("D") else 1
            newc = self.cnt[inc]
        self.q[eng].append((fn, inc, w))
        return (inc, newc) if inc else None

    def t(self, fn, wait=None):
        return self._emit("tensor", fn, "pe", wait)

    def v(self, fn, wait=None):
        return self._emit("vector", fn, "dve", wait)

    def s(self, fn, wait=None):
        return self._emit("scalar", fn, "act", wait)

    def g(self, fn, wait=None):
        return self._emit("gpsimd", fn, "gp", wait)

    def dma(self, fn, wait=None, q="sync", sem=None):
        return self._emit(q, fn, sem or ("Dsy" if q == "sync" else "Dgp"), wait)

    def dwait(self, tok, eng=None):
        s, cc = tok
        eng = eng or ("sync" if s == "Dsy" else "gpsimd")
        self._emit(eng, None, None, [tok])

    def mmgroup(self, entries):
        last = None
        nn = len(entries)
        for i, (fn, w) in enumerate(entries):
            last = self.t(lambda e, fn=fn, i=i: fn(e, i == 0, i == nn - 1), wait=w)
        return last

    def bar(self, engines=("tensor", "vector", "scalar", "gpsimd")):
        waits = [(s, cc) for s, cc in self.cnt.items()
                 if cc > 0 and not s.startswith("bar_")]
        for s, _ in waits:
            self.sem(s)
        for en in engines:
            self.q[en].append((None, None, list(waits)))

    def barall(self):
        self.bar(ENGINES)

    def seal(self):
        self.flush(self.blk)

    def flush(self, blk):
        for en in ENGINES:
            items = self.q[en]
            if not items:
                continue

            def runner(e, items=items):
                for fn, inc, waits in items:
                    for s, cc in waits:
                        e.wait_ge(self.sems[s], cc)
                    if fn is not None:
                        ins = fn(e)
                        if inc is not None:
                            ins.then_inc(self.sems[inc],
                                         16 if inc.startswith("D") else 1)
            getattr(blk, en)(runner)
            self.q[en] = []


class Ctx:
    pass


_UNIQ = [0]


def _sbt(nc, name, shape, dt):
    _UNIQ[0] += 1
    return nc.sbuf_tensor(f"{name}_{_UNIQ[0]}", shape, dt)


def _pst(nc, name, shape, dt):
    _UNIQ[0] += 1
    return nc.psum_tensor(f"{name}_{_UNIQ[0]}", shape, dt)



def make_ctx(nc, es, p, W, io_names):
    c = Ctx()
    c.nc, c.es, c.p, c.W = nc, es, p, W
    c.hbm = {}
    for name, arr in W.items():
        dt = BF16 if arr.dtype == BF else F32
        c.hbm[name] = nc.dram_tensor(name, list(arr.shape), dt, kind="ExternalInput").ap()
    for name, shape, kind in io_names:
        c.hbm[name] = nc.dram_tensor(name, shape, F32, kind=kind).ap()
    sb = lambda nm, sh, dt=F32: es.enter_context(_sbt(nc, "S_" + nm, sh, dt))
    c.sb = sb
    c.xT = [sb(f"xT{i}", [128, N]) for i in range(DC)]
    c.yT = [sb(f"yT{i}", [128, N], BF16) for i in range(DC)]
    c.ident = sb("ident", [128, 128], BF16)
    c.ones_col_bf = sb("ones_col_bf", [128, 1], BF16)
    c.ones_row = sb("ones_row", [1, 2048])
    c.ones_row_bf = sb("ones_row_bf", [1, 128], BF16)
    c.row1 = sb("row1", [1, 2048])
    c.negb = sb("negb", [128, 2])
    c.srec = sb("srec", [128, 1024])
    c.epscol = sb("epscol", [1, 1])
    return c


def load_consts(c):
    p = c.p
    p.v(lambda e: e.memset(c.epscol[:], 1e-5))
    tok = None
    for nm, dst in (("ident_bf", c.ident), ("ones_col_bf", c.ones_col_bf),
                    ("ones_row", c.ones_row), ("ones_row_bf", c.ones_row_bf)):
        tok = p.dma(lambda e, nm=nm, dst=dst: e.dma_start(out=dst[:], in_=c.hbm[nm]))
    p.dwait(tok)
    p.barall()


def load_slab(c, hbm_name, dst_tiles, row0, nrows, col0, cols, idx0=None, dcol0=0):
    p = c.p
    tok = None
    for i in range(nrows // 128):
        src = c.hbm[hbm_name]
        if idx0 is not None:
            src = src[idx0]
        tok = p.dma(lambda e, i=i, src=src: e.dma_start(
            out=dst_tiles[i][:, dcol0:dcol0 + cols],
            in_=src[row0 + 128 * i:row0 + 128 * (i + 1), col0:col0 + cols]), q="gpsimd")
    return tok


def load_small(c, hbm_name, dst_ap):
    return c.p.dma(lambda e, dst_ap=dst_ap: e.dma_start(out=dst_ap, in_=c.hbm[hbm_name]))


def emit_ln2(c, src, out, gcol=None, bcol=None):
    p, nc = c.p, c.nc
    src_bf = (src is c.yT)
    with scope(c) as _oses:
        row2 = _oses.enter_context(_sbt(nc, "lnrow2", [1, 2048], F32))
        row3 = _oses.enter_context(_sbt(nc, "lnrow3", [1, 2048], F32))
        _emit_ln2_body(c, src, out, gcol, bcol, row2, row3)


def _emit_ln2_body(c, src, out, gcol, bcol, row2, row3):
    p, nc = c.p, c.nc
    src_bf = (src is c.yT)
    with scope(c) as ses:
        E = ses.enter_context
        mirc = [E(_sbt(nc, f"ln_m{i}", [128, N], BF16)) for i in range(2)]
        sqc = [E(_sbt(nc, f"ln_s{i}", [128, N], BF16)) for i in range(2)]
        repR = E(_sbt(nc, "ln_repR", [128, N], F32))
        repMR = E(_sbt(nc, "ln_repMR", [128, N], F32))
        tscr = E(_sbt(nc, "ln_tscr", [128, N], F32))
        uscr = E(_sbt(nc, "ln_uscr", [128, N], F32)) if gcol is not None else None
        ps_sum = E(_pst(nc, "ln_sum", [1, N], F32))
        ps_sq = E(_pst(nc, "ln_sq", [1, N], F32))
        mm_done = {}
        for i in range(DC):
            b = i % 2
            if src_bf:
                mt = src[i]
                cp = None
            else:
                mt = mirc[b]
                cp = p.s(lambda e, i=i, b=b: e.activation(mirc[b][:], src[i][:], AF.Copy),
                         wait=[mm_done.get(i - 2)])
            sq = p.s(lambda e, mt=mt, b=b: e.activation(sqc[b][:], mt[:], AF.Square),
                     wait=[mm_done.get(i - 2)] if src_bf else None)
            last = None
            for nb in range(4):
                sl = slice(512 * nb, 512 * (nb + 1))
                p.t(lambda e, mt=mt, sl=sl, i=i: e.matmul(
                    ps_sum[:, sl], c.ones_col_bf[:], mt[:, sl],
                    start=(i == 0), stop=(i == DC - 1)), wait=[cp])
                last = p.t(lambda e, b=b, sl=sl, i=i: e.matmul(
                    ps_sq[:, sl], c.ones_col_bf[:], sqc[b][:, sl],
                    start=(i == 0), stop=(i == DC - 1)), wait=[sq])
            mm_done[i] = last
        p.bar()
        p.v(lambda e: e.tensor_scalar_mul(c.row1[:], ps_sum[:], 1.0 / 1024.0))
        p.bar(("vector", "scalar"))
        p.s(lambda e: e.activation(row2[:], c.row1[:], AF.Square))
        p.bar(("vector", "scalar"))
        p.v(lambda e: e.scalar_tensor_tensor(row3[:], ps_sq[:], 1.0 / 1024.0, row2[:],
                                             ALU.mult, ALU.subtract))
        p.bar(("vector", "scalar"))
        p.s(lambda e: e.activation(row2[:], row3[:], AF.Ln, bias=c.epscol[0:1, 0:1]))
        p.bar(("scalar",))
        p.s(lambda e: e.activation(row2[:], row2[:], AF.Exp, scale=-0.5))
        p.bar(("vector", "scalar"))
        p.v(lambda e: e.tensor_mul(row3[:], c.row1[:], row2[:]))
        p.bar()
    with scope(c) as ses:
        E = ses.enter_context
        repR = E(_sbt(nc, "ln_repR", [128, N], F32))
        repMR = E(_sbt(nc, "ln_repMR", [128, N], F32))
        tscr = E(_sbt(nc, "ln_tscr", [128, N], F32))
        uscr = E(_sbt(nc, "ln_uscr", [128, N], F32)) if gcol is not None else None
        ps_rep = E(_pst(nc, "ln_rep", [128, N], F32))
        for row, dst in ((row2, repR), (row3, repMR)):
            for nb in range(4):
                sl = slice(512 * nb, 512 * (nb + 1))
                p.t(lambda e, sl=sl, row=row: e.matmul(
                    ps_rep[:, sl], c.ones_row[0:1, 0:128], row[:, sl],
                    start=True, stop=True))
            p.bar(("tensor", "vector"))
            p.v(lambda e, dst=dst: e.tensor_copy(dst[:], ps_rep[:]))
            p.bar(("tensor", "vector"))
        for i in range(DC):
            st = src[i]
            dv = p.v(lambda e, st=st: e.tensor_mul(tscr[:], st[:], repR[:]))
            if gcol is None:
                p.g(lambda e, i=i: e.tensor_sub(out[i][:], tscr[:], repMR[:]), wait=[dv])
                p.bar(("vector", "gpsimd"))
            else:
                gv = p.g(lambda e: e.tensor_sub(uscr[:], tscr[:], repMR[:]), wait=[dv])
                p.v(lambda e, i=i: e.tensor_scalar(out[i][:], uscr[:],
                                                   gcol[:, i:i + 1], bcol[:, i:i + 1],
                                                   ALU.mult, ALU.add), wait=[gv])
                p.bar(("vector", "gpsimd", "scalar"))
    p.barall()


from contextlib import contextmanager


@contextmanager
def scope(c):
    es = ExitStack()
    try:
        yield es
    finally:
        c.p.seal()
        es.close()


def emit_negB(c, qtiles, ktiles, dh_total):
    p, nc = c.p, c.nc
    with scope(c) as ses:
        E = ses.enter_context
        sqb = E(_sbt(nc, "nb_sq", [128, N], BF16))
        r1 = E(_sbt(nc, "nb_r1", [1, 2], F32))
        psn = E(_pst(nc, "nb_ps", [1, N], F32))
        pb = E(_pst(nc, "nb_pb", [128, 2], F32))
        lastmm = None
        for j, tiles in enumerate((qtiles, ktiles)):
            ntl = len(tiles)
            for t, tl in enumerate(tiles):
                pp = tl.shape[0]
                sq = p.s(lambda e, tl=tl, pp=pp: e.activation(sqb[0:pp, :], tl[:],
                                                              AF.Square), wait=[lastmm])
                for nb in range(4):
                    sl = slice(512 * nb, 512 * (nb + 1))
                    lastmm = p.t(lambda e, pp=pp, sl=sl, t=t, ntl=ntl: e.matmul(
                        psn[:, sl], c.ones_col_bf[0:pp, 0:1], sqb[0:pp, sl],
                        start=(t == 0), stop=(t == ntl - 1)), wait=[sq])
            p.bar(("tensor", "vector"))
            p.v(lambda e, j=j: e.tensor_reduce(r1[:, j:j + 1], psn[:], AX.X, ALU.max))
            p.bar(("tensor", "vector"))
        p.v(lambda e: e.tensor_mul(r1[:, 0:1], r1[:, 0:1], r1[:, 1:2]))
        p.bar(("vector", "scalar"))
        p.s(lambda e: e.activation(r1[:, 0:1], r1[:, 0:1], AF.Ln))
        p.bar(("vector", "scalar"))
        p.s(lambda e: e.activation(r1[:, 0:1], r1[:, 0:1], AF.Exp, scale=0.5))
        p.bar(("vector", "scalar"))
        p.v(lambda e: e.tensor_scalar(r1[:, 0:1], r1[:, 0:1], 2.0, None, ALU.add))
        p.bar(("vector",))
        p.v(lambda e: e.tensor_scalar(r1[:, 1:2], r1[:, 0:1], NBIG, None, ALU.add))
        p.bar(("tensor", "vector"))
        for j in range(2):
            p.t(lambda e, j=j: e.matmul(pb[:, j:j + 1], c.ones_row[0:1, 0:128],
                                        r1[:, j:j + 1], start=True, stop=True))
        p.bar(("tensor", "vector"))
        p.v(lambda e: e.tensor_scalar_mul(c.negb[:], pb[:], -1.0))
    p.barall()


def emit_attn_std(c, cfg):
    p, nc = c.p, c.nc
    nh = cfg["nh"]
    HG, gd = 4, 256
    with scope(c) as ses:
        E = ses.enter_context
        qa = [E(_sbt(nc, f"aq{i}", [128, N], BF16)) for i in range(2)]
        ka = [E(_sbt(nc, f"ak{i}", [128, N], BF16)) for i in range(2)]
        ao = [E(_sbt(nc, f"ao{i}", [128, N], BF16)) for i in range(2)]
        va = [E(_sbt(nc, f"avv{i}", [128, HG * 65], BF16)) for i in range(NT)]
        pt = [E(_sbt(nc, f"apt{i}", [128, 1024], BF16)) for i in range(2)]
        wsl = [E(_sbt(nc, f"awr{i}", [128, gd], BF16)) for i in range(DC)]
        wos = [E(_sbt(nc, f"awo{i}", [128, D], BF16)) for i in range(2)]
        brow = E(_sbt(nc, "abrow", [1, D], F32))
        bvrow = E(_sbt(nc, "abvrow", [1, D], F32))
        bqcol = E(_sbt(nc, "abqcol", [128, 8], F32))
        bkcol = E(_sbt(nc, "abkcol", [128, 8], F32))
        mstep = E(_sbt(nc, "amstep", [128, 4992], BF16)) if cfg["causal"] else None
        tok = None
        if cfg["causal"]:
            tok = load_small(c, "mask_step", mstep[:])
        if cfg.get("bout_row"):
            tok = load_small(c, cfg["bout_row"], brow[:])
        if cfg.get("bv_row"):
            tok = load_small(c, cfg["bv_row"], bvrow[:])
        if cfg.get("bq_col"):
            tok = load_small(c, cfg["bq_col"], bqcol[:])
            tok = load_small(c, cfg["bk_col"], bkcol[:])
        if tok is not None:
            p.dwait(tok)
        for i in range(NT):
            p.v(lambda e, i=i: e.memset(
                va[i][:].rearrange("p (h x) -> p h x", x=65)[:, :, 64:65], 1.0))
        p.barall()

        for g in range(nh // HG):
            for kind in ("q", "k"):
                dst = qa if kind == "q" else ka
                sc = cfg["scale"] if kind == "q" else 1.0
                bcol = None
                if cfg.get("b" + kind + "_col"):
                    bcol = bqcol if kind == "q" else bkcol
                tok = load_slab(c, cfg["w" + kind], wsl, 0, D,
                                cfg[kind + "off"] + gd * g, gd)
                p.dwait(tok)
                p.barall()
                with scope(c) as ps_es:
                    pps = [ps_es.enter_context(_pst(nc, f"aqp{i}", [128, 512], F32))
                           for i in range(2)]
                    done = {}
                    for j in range(2 * 4):
                        oc, nb = j // 4, j % 4
                        b = j % 2
                        sl = slice(512 * nb, 512 * (nb + 1))
                        ents = []
                        for i in range(DC):
                            ents.append((
                                lambda e, st, sp, i=i, oc=oc, sl=sl, b=b: e.matmul(
                                    pps[b][:], wsl[i][:, 128 * oc:128 * (oc + 1)],
                                    c.yT[i][:, sl], start=st, stop=sp),
                                [done.get(j - 2)] if i == 0 else None))
                        last = p.mmgroup(ents)
                        colj = (2 * g + oc)
                        if bcol is not None:
                            done[j] = p.v(lambda e, oc=oc, sl=sl, b=b, colj=colj, dst=dst,
                                          sc=sc, bcol=bcol: e.tensor_scalar(
                                dst[oc][:, sl], pps[b][:], bcol[:, colj:colj + 1], sc,
                                ALU.add, ALU.mult), wait=[last])
                        else:
                            done[j] = p.s(lambda e, oc=oc, sl=sl, b=b, dst=dst,
                                          sc=sc: e.activation(
                                dst[oc][:, sl], pps[b][:], AF.Copy, scale=sc), wait=[last])
                p.barall()
            tok = load_slab(c, cfg["wv"], wsl, 0, D, cfg["voff"] + gd * g, gd)
            p.dwait(tok)
            p.barall()
            with scope(c) as vses:
                vps = [vses.enter_context(_pst(nc, f"avp{i}", [128, 256], F32))
                       for i in range(2)]
                vdone = {}
                for nt in range(NT):
                    b = nt % 2
                    ents = []
                    for i in range(DC):
                        ents.append((
                            lambda e, st, sp, i=i, nt=nt, b=b: e.matmul(
                                vps[b][:], c.yT[i][:, 128 * nt:128 * (nt + 1)],
                                wsl[i][:, 0:256], start=st, stop=sp),
                            [vdone.get(nt - 2)] if i == 0 else None))
                    if cfg.get("bv_row"):
                        ents.append((
                            lambda e, st, sp, b=b, g=g: e.matmul(
                                vps[b][:], c.ones_row[0:1, 0:128],
                                bvrow[0:1, gd * g:gd * (g + 1)], start=st, stop=sp), None))
                    last = p.mmgroup(ents)
                    vdone[nt] = p.v(lambda e, nt=nt, b=b: e.tensor_copy(
                        va[nt][:].rearrange("p (h x) -> p h x", x=65)[:, :, 0:64],
                        vps[b][:].rearrange("p (h x) -> p h x", x=64)), wait=[last])
            p.barall()
            emit_negB(c, qa, ka, 64)
            with scope(c) as hses:
                sps = [hses.enter_context(_pst(nc, f"asp{i}", [128, 1024], F32))
                       for i in range(2)]
                ops = hses.enter_context(_pst(nc, "aop", [65, 1024], F32))
                rps = hses.enter_context(_pst(nc, "arp", [64, 1024], F32))
                expd = {}
                prev_ml = prev_rp = prev_cp = None
                for hh in range(HG):
                    ti, ro = (hh * 64) // 128, (hh * 64) % 128
                    for qb in range(2):
                        q0 = 1024 * qb
                        qsl = slice(q0, q0 + 1024)
                        kts = [kt for kt in range(NT)
                               if (not cfg["causal"]) or 128 * kt <= q0 + 1023]
                        avd = {}
                        for idx, kt in enumerate(kts):
                            b = idx % 2
                            mm = None
                            for hf in range(2):
                                mm = p.t(lambda e, kt=kt, b=b, ti=ti, ro=ro, q0=q0,
                                         hf=hf: e.matmul(
                                    sps[b][:, 512 * hf:512 * (hf + 1)],
                                    ka[ti][ro:ro + 64, 128 * kt:128 * (kt + 1)],
                                    qa[ti][ro:ro + 64,
                                           q0 + 512 * hf:q0 + 512 * (hf + 1)],
                                    start=True, stop=True),
                                    wait=([expd.get(idx - 2)] if hf == 0 else None))
                            ex = p.s(lambda e, b=b: e.activation(
                                pt[b][:], sps[b][:], AF.Exp, bias=c.negb[:, 0:1]),
                                wait=[mm, avd.get(idx - 2)])
                            expd[idx] = ex
                            wv = [ex]
                            if cfg["causal"] and 128 * kt + 127 > q0:
                                u0 = STEP_OFF + q0 - 128 * kt
                                dv = p.v(lambda e, b=b, u0=u0: e.tensor_mul(
                                    pt[b][:], pt[b][:], mstep[:, u0:u0 + 1024]),
                                    wait=[ex])
                                expd[idx] = dv
                                wv = [dv]
                            av = None
                            for hf in range(2):
                                av = p.t(lambda e, kt=kt, b=b, hh=hh, idx=idx, hf=hf,
                                         nn=len(kts): e.matmul(
                                    ops[:, 512 * hf:512 * (hf + 1)],
                                    va[kt][:, 65 * hh:65 * hh + 65],
                                    pt[b][:, 512 * hf:512 * (hf + 1)],
                                    start=(idx == 0), stop=(idx == nn - 1)),
                                    wait=(wv + [prev_ml] if (idx == 0 and hf == 0)
                                          else wv))
                            avd[idx] = av
                        expd = {-2: expd.get(len(kts) - 2), -1: expd.get(len(kts) - 1)}
                        rc = p.v(lambda e: e.reciprocal(c.row1[0:1, 0:1024],
                                                        ops[64:65, :]),
                                 wait=[avd[len(kts) - 1], prev_rp])
                        rp = None
                        for hf in range(2):
                            rp = p.t(lambda e, hf=hf: e.matmul(
                                rps[:, 512 * hf:512 * (hf + 1)],
                                c.ones_row[0:1, 0:64],
                                c.row1[0:1, 512 * hf:512 * (hf + 1)],
                                start=True, stop=True),
                                wait=([rc, prev_cp] if hf == 0 else None))
                        prev_rp = rp
                        cp = p.v(lambda e: e.tensor_copy(c.srec[0:64, 0:1024], rps[:]),
                                 wait=[rp, prev_ml])
                        prev_cp = cp
                        prev_ml = p.v(lambda e, ti=ti, ro=ro, qsl=qsl: e.tensor_mul(
                            ao[ti][ro:ro + 64, qsl], ops[0:64, :],
                            c.srec[0:64, 0:1024]), wait=[cp])
            tok = None
            for i in range(2):
                tok = p.dma(lambda e, i=i, g=g: e.dma_start(
                    out=wos[i][:], in_=c.hbm[cfg["wo"]][gd * g + 128 * i:
                                                        gd * g + 128 * (i + 1), :]),
                    q="gpsimd")
            p.dwait(tok)
            p.barall()
            with scope(c) as oses:
                pps = [oses.enter_context(_pst(nc, f"apo{i}", [128, 512], F32))
                       for i in range(2)]
                done = {}
                for j in range(DC * 4):
                    oc, nb = j // 4, j % 4
                    b = j % 2
                    sl = slice(512 * nb, 512 * (nb + 1))
                    ents = []
                    for i in range(2):
                        ents.append((
                            lambda e, st, sp, i=i, oc=oc, sl=sl, b=b: e.matmul(
                                pps[b][:], wos[i][:, 128 * oc:128 * (oc + 1)],
                                ao[i][:, sl], start=st, stop=sp),
                            [done.get(j - 2)] if i == 0 else None))
                    if g == 0 and cfg.get("bout_row"):
                        ents.append((lambda e, st, sp, oc=oc, b=b: e.matmul(
                            pps[b][:], brow[0:1, 128 * oc:128 * (oc + 1)],
                            c.ones_row[0:1, 0:512], start=st, stop=sp), None))
                    if g == 0 and cfg.get("add_y"):
                        ents.append((lambda e, st, sp, oc=oc, sl=sl, b=b: e.matmul(
                            pps[b][:], c.ident[:], c.yT[oc][:, sl], start=st, stop=sp),
                            None))
                    last = p.mmgroup(ents)
                    done[j] = p.v(lambda e, oc=oc, sl=sl, b=b: e.tensor_add(
                        c.xT[oc][:, sl], c.xT[oc][:, sl], pps[b][:]), wait=[last])
            p.barall()


def emit_moe(c):
    p, nc = c.p, c.nc
    with scope(c) as ses:
        E = ses.enter_context
        gatesT = E(_sbt(nc, "mgT", [8, N], BF16))
        grep = E(_sbt(nc, "mgrep", [128, N], F32))
        xe = [E(_sbt(nc, f"mxe{i}", [128, N], BF16)) for i in range(DC)]
        wsl = [E(_sbt(nc, f"mw{i}", [128, 512], BF16)) for i in range(DC)]
        w3sl = [E(_sbt(nc, f"mw3{i}", [128, 1024], BF16)) for i in range(DC)]
        rsl = [E(_sbt(nc, f"mr{i}", [128, 8], BF16)) for i in range(DC)]
        silu = E(_sbt(nc, "msil", [128, 512], F32))
        uscr = E(_sbt(nc, "musc", [128, 512], F32))
        erow = E(_sbt(nc, "mer", [1, 8], F32))
        ebrep = E(_sbt(nc, "meb", [128, 8], F32))
        sc1 = E(_sbt(nc, "msc1", [128, 8], F32))
        sc2 = E(_sbt(nc, "msc2", [128, 8], F32))
        top8 = E(_sbt(nc, "mtop", [128, 8], F32))
        gt = E(_sbt(nc, "mgt", [128, 8], BF16))
        sel8 = E(_sbt(nc, "msel", [8, 1024], BF16))
        tok = load_small(c, "sel8", sel8[:])
        tok = load_slab(c, "moe_routerT", rsl, 0, D, 0, 8)
        tok2 = load_small(c, "moe_ebias_row", erow[:])
        p.dwait(tok)
        p.dwait(tok2)
        p.barall()
        with scope(c) as rses:
            rps = rses.enter_context(_pst(nc, "mrp", [128, 8], F32))
            eps = rses.enter_context(_pst(nc, "mep", [128, 8], F32))
            gps = rses.enter_context(_pst(nc, "mgp", [8, 128], BF16))
            p.t(lambda e: e.matmul(eps[:], c.ones_row[0:1, 0:128], erow[0:1, :],
                                   start=True, stop=True))
            p.bar(("tensor", "vector"))
            p.v(lambda e: e.tensor_copy(ebrep[:], eps[:]))
            p.bar(("tensor", "vector"))
            prev_sg = prev_gm = prev_cp = prev_tr = None
            for nt in range(NT):
                ents = []
                for i in range(DC):
                    ents.append((lambda e, st, sp, i=i, nt=nt: e.matmul(
                        rps[:], c.yT[i][:, 128 * nt:128 * (nt + 1)], rsl[i][:, 0:8],
                        start=st, stop=sp),
                        ([prev_sg] if i == 0 else None)))
                last = p.mmgroup(ents)
                sg = p.s(lambda e: e.activation(sc1[:], rps[:], AF.Sigmoid),
                         wait=[last, prev_gm])
                prev_sg = sg
                p.v(lambda e: e.tensor_add(sc2[:], sc1[:], ebrep[:]), wait=[sg])
                p.v(lambda e: e.max(top8[:], sc2[:]))
                p.v(lambda e: e.tensor_scalar(sc2[:], sc2[:], top8[:, 1:2], None,
                                              ALU.is_ge))
                gm = p.v(lambda e: e.tensor_mul(gt[:], sc2[:], sc1[:]), wait=[prev_tr])
                prev_gm = gm
                tr = p.t(lambda e: e.transpose(gps[:], gt[:], c.ident[:]),
                         wait=[gm, prev_cp])
                prev_tr = tr
                prev_cp = p.v(lambda e, nt=nt: e.tensor_copy(
                    gatesT[:, 128 * nt:128 * (nt + 1)], gps[:]), wait=[tr])
            p.bar()
        p.barall()
        for ex in range(12):
            gated = ex < 8
            if gated:
                with scope(c) as gses:
                    gpps = gses.enter_context(_pst(nc, "mgrp", [128, 512], F32))
                    for nb in range(4):
                        sl = slice(512 * nb, 512 * (nb + 1))
                        mmg = p.t(lambda e, sl=sl, ex=ex: e.matmul(
                            gpps[:], sel8[:, 128 * ex:128 * (ex + 1)], gatesT[:, sl],
                            start=True, stop=True))
                        p.v(lambda e, sl=sl: e.tensor_copy(grep[:, sl], gpps[:]),
                            wait=[mmg])
                        p.bar(("tensor", "vector"))
            tokw3 = None
            for i in range(DC):
                tokw3 = p.dma(lambda e, i=i, ex=ex: e.dma_start(
                    out=w3sl[i][:], in_=c.hbm["moe_w3T"][ex, 128 * i:128 * (i + 1), :]),
                    q="gpsimd", sem="Dgw")

            with scope(c) as ases:
                pa1 = [ases.enter_context(_pst(nc, f"mpa1{i}", [128, 512], F32))
                       for i in range(2)]
                pa2 = [ases.enter_context(_pst(nc, f"mpa2{i}", [128, 512], F32))
                       for i in range(2)]
                ldtok = {}
                a1rd = {}
                a2rd = {}
                xvd = {}
                slab_rd = {}
                for ch in range(9):
                    if ch < 8:
                        tok = None
                        for i in range(DC):
                            tok = p.dma(lambda e, i=i, ch=ch, ex=ex: e.dma_start(
                                out=wsl[i][:, 256 * (ch % 2):256 * (ch % 2) + 256],
                                in_=c.hbm["moe_w12T"][ex, 128 * i:128 * (i + 1),
                                                      256 * ch:256 * (ch + 1)]),
                                q="gpsimd", sem=("Dga" if ch % 2 == 0 else "Dgb"),
                                wait=([slab_rd.get(ch - 2)] if i == 0 else None))
                        ldtok[ch] = tok
                    if ch == 0:
                        continue
                    cc = ch - 1
                    co = 256 * (cc % 2)
                    for nb in range(4):
                        j = cc * 4 + nb
                        b = j % 2
                        sl = slice(512 * nb, 512 * (nb + 1))
                        ents1 = [(lambda e, st, sp, i=i, sl=sl, b=b, co=co: e.matmul(
                            pa1[b][:], wsl[i][:, co:co + 128], c.yT[i][:, sl],
                            start=st, stop=sp),
                            ([ldtok[cc], a1rd.get(j - 2)] if i == 0 else None))
                            for i in range(DC)]
                        l1 = p.mmgroup(ents1)
                        ents2 = [(lambda e, st, sp, i=i, sl=sl, b=b, co=co: e.matmul(
                            pa2[b][:], wsl[i][:, co + 128:co + 256], c.yT[i][:, sl],
                            start=st, stop=sp),
                            ([a2rd.get(j - 2)] if i == 0 else None))
                            for i in range(DC)]
                        l2 = p.mmgroup(ents2)
                        sv = p.s(lambda e, b=b: e.activation(silu[:], pa1[b][:],
                                                             AF.Silu),
                                 wait=[l1, a2rd.get(j - 1)])
                        a1rd[j] = sv
                        if gated:
                            uv = p.v(lambda e, b=b: e.tensor_mul(uscr[:], pa2[b][:],
                                                                 silu[:]),
                                     wait=[l2, sv, xvd.get(j - 1)])
                            xv = p.g(lambda e, cc=cc, sl=sl: e.tensor_mul(
                                xe[cc][:, sl], uscr[:], grep[:, sl]), wait=[uv])
                            a2rd[j] = uv
                            xvd[j] = xv
                        else:
                            xv = p.v(lambda e, cc=cc, sl=sl, b=b: e.scalar_tensor_tensor(
                                xe[cc][:, sl], pa2[b][:], 1.0, silu[:],
                                ALU.mult, ALU.mult), wait=[l2, sv])
                            a2rd[j] = xv
                            xvd[j] = xv
                        slab_rd[cc] = l2
            p.dwait(tokw3, eng="gpsimd")
            p.barall()
            with scope(c) as rses:
                pr = [rses.enter_context(_pst(nc, f"mpr{i}", [128, 512], F32))
                      for i in range(2)]
                done = {}
                for j in range(DC * 4):
                    dcb, nb = j // 4, j % 4
                    b = j % 2
                    sl = slice(512 * nb, 512 * (nb + 1))
                    ents = [(lambda e, st, sp, ch=ch2, dcb=dcb, sl=sl, b=b: e.matmul(
                        pr[b][:], w3sl[ch][:, 128 * dcb:128 * (dcb + 1)], xe[ch][:, sl],
                        start=st, stop=sp),
                        [done.get(j - 2)] if ch2 == 0 else None) for ch2 in range(DC)]
                    last = p.mmgroup(ents)
                    done[j] = p.v(lambda e, dcb=dcb, sl=sl, b=b: e.tensor_add(
                        c.xT[dcb][:, sl], c.xT[dcb][:, sl], pr[b][:]), wait=[last])
            p.barall()


def emit_outproj(c, wo_name, ao, rows0, nrows, with_bias_row=None, add_y=False):
    p, nc = c.p, c.nc
    nch = nrows // 128
    with scope(c) as ses:
        E = ses.enter_context
        wos = [E(_sbt(nc, f"opw{i}", [128, D], BF16)) for i in range(nch)]
        brow = E(_sbt(nc, "opbr", [1, D], F32))
        tok = None
        for i in range(nch):
            tok = p.dma(lambda e, i=i: e.dma_start(
                out=wos[i][:], in_=c.hbm[wo_name][rows0 + 128 * i:rows0 + 128 * (i + 1), :]),
                q="gpsimd")
        if with_bias_row:
            tok2 = load_small(c, with_bias_row, brow[:])
            p.dwait(tok2)
        p.dwait(tok)
        p.barall()
        pps = [E(_pst(nc, f"opp{i}", [128, 512], F32)) for i in range(2)]
        done = {}
        for j in range(DC * 4):
            oc, nb = j // 4, j % 4
            b = j % 2
            sl = slice(512 * nb, 512 * (nb + 1))
            ents = []
            for i in range(nch):
                ents.append((lambda e, st, sp, i=i, oc=oc, sl=sl, b=b: e.matmul(
                    pps[b][:], wos[i][:, 128 * oc:128 * (oc + 1)], ao[i][:, sl],
                    start=st, stop=sp), [done.get(j - 2)] if i == 0 else None))
            if with_bias_row:
                ents.append((lambda e, st, sp, oc=oc, b=b: e.matmul(
                    pps[b][:], brow[0:1, 128 * oc:128 * (oc + 1)],
                    c.ones_row[0:1, 0:512], start=st, stop=sp), None))
            if add_y:
                ents.append((lambda e, st, sp, oc=oc, sl=sl, b=b: e.matmul(
                    pps[b][:], c.ident[:], c.yT[oc][:, sl], start=st, stop=sp), None))
            last = p.mmgroup(ents)
            done[j] = p.v(lambda e, oc=oc, sl=sl, b=b: e.tensor_add(
                c.xT[oc][:, sl], c.xT[oc][:, sl], pps[b][:]), wait=[last])
    p.barall()


L0B_LIMIT = "full"


def emit_l0b(c):
    p, nc = c.p, c.nc
    SC = 1.0 / 8.0
    HG, gd = 4, 256
    with scope(c) as ses:
        E = ses.enter_context
        qr = [E(_sbt(nc, f"bq{i}", [128, N], BF16)) for i in range(2)]
        kr = [E(_sbt(nc, f"bk{i}", [128, N], BF16)) for i in range(2)]
        qs1 = E(_sbt(nc, "bs1t", [128, N], BF16))
        qs = [qs1, qs1]
        ao = [E(_sbt(nc, f"bo{i}", [128, N], BF16)) for i in range(2)]
        va = [E(_sbt(nc, f"bv{i}", [128, HG * 65], BF16)) for i in range(NT)]
        pt = [E(_sbt(nc, f"bp{i}", [128, 1024], BF16)) for i in range(2)]
        wsl = [E(_sbt(nc, f"bw{i}", [128, gd], BF16)) for i in range(DC)]
        bselT = [E(_sbt(nc, f"bbt{i}", [128, N], BF16)) for i in range(2)]
        cos2 = E(_sbt(nc, "bcos", [128, 2688], BF16))
        sin2 = E(_sbt(nc, "bsin", [128, 2560], BF16))
        pisb = E(_sbt(nc, "bpi", [128, 128], BF16))
        kl2 = E(_sbt(nc, "bkl", [128, N], BF16))
        eb2 = E(_sbt(nc, "beb", [128, N], BF16))
        mband = cos2
        mstep = sin2
        k1sb = E(_sbt(nc, "bk1", [128, 256], BF16))
        top8 = E(_sbt(nc, "bt8", [128, 8], F32))
        bsel = E(_sbt(nc, "bbs", [128, 256], BF16))
        load_small(c, "pi128T", pisb[:])
        tok = load_small(c, "ebig2", eb2[:])
        tokg = load_slab(c, "l0b_klod1gT2", [kl2], 0, 128, 0, 2048)
        p.dwait(tok)
        p.dwait(tokg)
        for i in range(NT):
            p.v(lambda e, i=i: e.memset(
                va[i][:].rearrange("p (h x) -> p h x", x=65)[:, :, 64:65], 1.0))
        p.barall()

        for g in range(4):
            load_cast(c, "rope_cos2", cos2[:, 0:N])
            tok = load_cast(c, "rope_sin2", sin2[:, 0:N])
            p.dwait(tok)
            p.barall()
            for kind in ("q", "k"):
                dst = qr if kind == "q" else kr
                sc = SC if kind == "q" else 1.0
                tok = load_slab(c, "l0b_w" + kind + "T", wsl, 0, D, gd * g, gd)
                p.dwait(tok)
                p.barall()
                with scope(c) as ps_es:
                    pps = [ps_es.enter_context(_pst(nc, f"bqp{i}", [128, 512], F32))
                           for i in range(2)]
                    pps2 = [ps_es.enter_context(_pst(nc, f"bqq{i}", [128, 512], F32))
                            for i in range(2)]
                    done = {}
                    pdone = {}
                    for j in range(2 * 4):
                        oc, nb = j // 4, j % 4
                        b = j % 2
                        sl = slice(512 * nb, 512 * (nb + 1))
                        ents = []
                        for i in range(DC):
                            ents.append((lambda e, st, sp, i=i, oc=oc, sl=sl, b=b:
                                         e.matmul(pps[b][:],
                                                  wsl[i][:, 128 * oc:128 * (oc + 1)],
                                                  c.yT[i][:, sl], start=st, stop=sp),
                                         [done.get(j - 2)] if i == 0 else None))
                        last = p.mmgroup(ents)
                        p.v(lambda e, oc=oc, sl=sl, b=b, sc=sc, dst=dst:
                            e.scalar_tensor_tensor(dst[oc][:, sl], pps[b][:], sc,
                                                   cos2[:, sl], ALU.mult, ALU.mult),
                            wait=[last])
                        e2 = p.v(lambda e, oc=oc, sl=sl, b=b, sc=sc:
                                 e.scalar_tensor_tensor(qs[oc][:, sl], pps[b][:], sc,
                                                        sin2[:, sl], ALU.mult, ALU.mult),
                                 wait=[pdone.get(j - 4)])
                        done[j] = e2
                        pm = p.t(lambda e, oc=oc, sl=sl, b=b: e.matmul(
                            pps2[b][:], pisb[:], qs[oc][:, sl], start=True, stop=True),
                            wait=[e2, pdone.get(j - 2)])
                        pdone[j] = p.v(lambda e, oc=oc, sl=sl, b=b, dst=dst: e.tensor_add(
                            dst[oc][:, sl], dst[oc][:, sl], pps2[b][:]), wait=[pm])
                p.barall()
            tok = load_slab(c, "l0b_wvT", wsl, 0, D, gd * g, gd)
            p.dwait(tok)
            p.barall()
            with scope(c) as vses:
                vps = [vses.enter_context(_pst(nc, f"bvp{i}", [128, 256], F32))
                       for i in range(2)]
                vdone = {}
                for nt in range(NT):
                    b = nt % 2
                    ents = []
                    for i in range(DC):
                        ents.append((lambda e, st, sp, i=i, nt=nt, b=b: e.matmul(
                            vps[b][:], c.yT[i][:, 128 * nt:128 * (nt + 1)],
                            wsl[i][:, 0:256], start=st, stop=sp),
                            [vdone.get(nt - 2)] if i == 0 else None))
                    last = p.mmgroup(ents)
                    vdone[nt] = p.v(lambda e, nt=nt, b=b: e.tensor_copy(
                        va[nt][:].rearrange("p (h x) -> p h x", x=65)[:, :, 0:64],
                        vps[b][:].rearrange("p (h x) -> p h x", x=64)), wait=[last])
            p.barall()
            if L0B_LIMIT == "qk":
                continue
            with scope(c) as kses:
                k1ps = kses.enter_context(_pst(nc, "bkp", [64, 64], F32))
                for h4 in range(4):
                    ti, ro = h4 // 2, 64 * (h4 % 2)
                    ents = []
                    for gg in range(32):
                        ents.append((lambda e, st, sp, gg=gg, ti=ti, ro=ro: e.matmul(
                            k1ps[:], kl2[ro:ro + 64, 64 * gg:64 * (gg + 1)],
                            kr[ti][ro:ro + 64, :].rearrange(
                                "p (t g) -> p g t", g=32)[:, gg, :],
                            start=st, stop=sp), None))
                    last = p.mmgroup(ents)
                    p.v(lambda e, h4=h4, ro=ro: e.tensor_copy(
                        k1sb[ro:ro + 64, 64 * h4:64 * (h4 + 1)], k1ps[:]), wait=[last])
                    p.bar(("tensor", "vector"))
            p.barall()
            if L0B_LIMIT == "pool":
                continue
            with scope(c) as sses:
                s1ps = [sses.enter_context(_pst(nc, f"bsp{i}", [128, 64], F32))
                        for i in range(4)]
                btps = [sses.enter_context(_pst(nc, f"btp{i}", [64, 128], BF16))
                        for i in range(2)]
                btcp = {}
                bsld = {}
                for nt in range(NT):
                    lasts = None
                    for h4 in range(4):
                        ti, ro = h4 // 2, 64 * (h4 % 2)
                        lasts = p.t(lambda e, h4=h4, ti=ti, ro=ro, nt=nt: e.matmul(
                            s1ps[h4][:],
                            qr[ti][ro:ro + 64, 128 * nt:128 * (nt + 1)],
                            k1sb[ro:ro + 64, 64 * h4:64 * (h4 + 1)],
                            start=True, stop=True), wait=[bsld.get(h4)])
                    bsl = None
                    for h4 in range(4):
                        p.v(lambda e, h4=h4: e.max(top8[:], s1ps[h4][:]), wait=[lasts])
                        bsl = p.v(lambda e, h4=h4: e.tensor_scalar(
                            bsel[:, 64 * h4:64 * (h4 + 1)], s1ps[h4][:],
                            top8[:, 7:8], None, ALU.is_ge))
                        bsld[h4] = bsl
                    for h4 in range(4):
                        ti, ro = h4 // 2, 64 * (h4 % 2)
                        b = h4 % 2
                        tr = p.t(lambda e, h4=h4, b=b: e.transpose(
                            btps[b][:], bsel[:, 64 * h4:64 * (h4 + 1)], c.ident[:]),
                            wait=[bsl, btcp.get(h4 - 2)])
                        btcp[h4] = p.v(lambda e, ti=ti, ro=ro, nt=nt, b=b: e.tensor_copy(
                            bselT[ti][ro:ro + 64, 128 * nt:128 * (nt + 1)], btps[b][:]),
                            wait=[tr])
                    btcp = {-2: btcp.get(2), -1: btcp.get(3)}
            p.barall()
            if L0B_LIMIT == "sel":
                continue
            emit_negB(c, qr, kr, 64)
            p.dma(lambda e: e.dma_start(out=mband[:],
                                        in_=c.hbm["mask_band"][:, 1024:3712]),
                  q="gpsimd")
            tok = p.dma(lambda e: e.dma_start(out=mstep[:],
                                              in_=c.hbm["mask_step"][:, 1408:3968]),
                        q="gpsimd")
            p.dwait(tok)
            p.barall()
            with scope(c) as hses:
                sps = [hses.enter_context(_pst(nc, f"bss{i}", [128, 1024], F32))
                       for i in range(2)]
                ops = hses.enter_context(_pst(nc, "bop", [65, 1024], F32))
                rps = hses.enter_context(_pst(nc, "brp", [64, 1024], F32))
                expd = {}
                prev_ml = prev_rp = prev_cp = None
                for h4 in range(4):
                    ti, ro = h4 // 2, 64 * (h4 % 2)
                    for qb in range(2):
                        q0 = 1024 * qb
                        qsl = slice(q0, q0 + 1024)
                        tiles = []
                        for kt in range(NT):
                            k0 = 128 * kt
                            if k0 <= q0 + 1023 and k0 + 127 >= q0 - 638:
                                tiles.append((kt, "band", False))
                        for kt in range(NT):
                            k0 = 128 * kt
                            if k0 <= q0 + 511:
                                tiles.append((kt, "below", k0 <= q0 - 639))
                        avd = {}
                        for idx, tspec in enumerate(tiles):
                            kt, mode, pure = tspec
                            k0 = 128 * kt
                            b = idx % 2
                            mm = None
                            for hf in range(2):
                                if mode == "band":
                                    mm = p.t(lambda e, kt=kt, b=b, ti=ti, ro=ro, q0=q0,
                                             hf=hf: e.matmul(
                                        sps[b][:, 512 * hf:512 * (hf + 1)],
                                        kr[ti][ro:ro + 64, 128 * kt:128 * (kt + 1)],
                                        qr[ti][ro:ro + 64,
                                               q0 + 512 * hf:q0 + 512 * (hf + 1)],
                                        start=True, stop=True),
                                        wait=([expd.get(idx - 2)] if hf == 0 else None))
                                else:
                                    p.t(lambda e, kt=kt, b=b, ti=ti, ro=ro, q0=q0,
                                        hf=hf: e.matmul(
                                        sps[b][:, 512 * hf:512 * (hf + 1)],
                                        kr[ti][ro:ro + 64, 128 * kt:128 * (kt + 1)],
                                        qr[ti][ro:ro + 64,
                                               q0 + 512 * hf:q0 + 512 * (hf + 1)],
                                        start=True, stop=False),
                                        wait=([expd.get(idx - 2)] if hf == 0 else None))
                                    mm = p.t(lambda e, kt=kt, b=b, ti=ti, ro=ro, q0=q0,
                                             hf=hf: e.matmul(
                                        sps[b][:, 512 * hf:512 * (hf + 1)],
                                        eb2[ro:ro + 64, 128 * kt:128 * (kt + 1)],
                                        bselT[ti][ro:ro + 64,
                                                  q0 + 512 * hf:q0 + 512 * (hf + 1)],
                                        start=False, stop=True))
                            bias = c.negb[:, 0:1] if mode == "band" else c.negb[:, 1:2]
                            ex = p.s(lambda e, b=b, bias=bias: e.activation(
                                pt[b][:], sps[b][:], AF.Exp, bias=bias),
                                wait=[mm, avd.get(idx - 2)])
                            expd[idx] = ex
                            wv = [ex]
                            if mode == "band":
                                u0 = 1024 + q0 - k0
                                dv = p.v(lambda e, b=b, u0=u0: e.tensor_mul(
                                    pt[b][:], pt[b][:], mband[:, u0:u0 + 1024]),
                                    wait=[ex])
                                expd[idx] = dv
                                wv = [dv]
                            elif not pure:
                                u0 = STEP_OFF + q0 - k0 - 512 - 1408
                                dv = p.v(lambda e, b=b, u0=u0: e.tensor_mul(
                                    pt[b][:], pt[b][:], mstep[:, u0:u0 + 1024]),
                                    wait=[ex])
                                expd[idx] = dv
                                wv = [dv]
                            av = None
                            for hf in range(2):
                                av = p.t(lambda e, kt=kt, b=b, h4=h4, idx=idx, hf=hf,
                                         nn=len(tiles): e.matmul(
                                    ops[:, 512 * hf:512 * (hf + 1)],
                                    va[kt][:, 65 * h4:65 * h4 + 65],
                                    pt[b][:, 512 * hf:512 * (hf + 1)],
                                    start=(idx == 0), stop=(idx == nn - 1)),
                                    wait=(wv + [prev_ml] if (idx == 0 and hf == 0)
                                          else wv))
                            avd[idx] = av
                        expd = {-2: expd.get(len(tiles) - 2),
                                -1: expd.get(len(tiles) - 1)}
                        rc = p.v(lambda e: e.reciprocal(c.row1[0:1, 0:1024],
                                                        ops[64:65, :]),
                                 wait=[avd[len(tiles) - 1], prev_rp])
                        rp = None
                        for hf in range(2):
                            rp = p.t(lambda e, hf=hf: e.matmul(
                                rps[:, 512 * hf:512 * (hf + 1)],
                                c.ones_row[0:1, 0:64],
                                c.row1[0:1, 512 * hf:512 * (hf + 1)],
                                start=True, stop=True),
                                wait=([rc, prev_cp] if hf == 0 else None))
                        prev_rp = rp
                        cp = p.v(lambda e: e.tensor_copy(c.srec[0:64, 0:1024], rps[:]),
                                 wait=[rp, prev_ml])
                        prev_cp = cp
                        prev_ml = p.v(lambda e, ti=ti, ro=ro, qsl=qsl: e.tensor_mul(
                            ao[ti][ro:ro + 64, qsl], ops[0:64, :],
                            c.srec[0:64, 0:1024]), wait=[cp])
            p.barall()
            if L0B_LIMIT == "heads":
                continue
            emit_outproj(c, "l0b_woT", ao, gd * g, gd)


def load_cast(c, hbm_name, dst_ap, idx0=None):
    src = c.hbm[hbm_name]
    if idx0 is not None:
        src = src[idx0]
    return c.p.dma(lambda e, src=src, dst_ap=dst_ap: e.dma_start(out=dst_ap, in_=src),
                   q="gpsimd")


def emit_l1a(c):
    p, nc = c.p, c.nc
    SC = float(1.0 / np.sqrt(160.0))
    HG = 2
    with scope(c) as ses:
        E = ses.enter_context
        cq = E(_sbt(nc, "lcq", [33, N], BF16))
        ckv = E(_sbt(nc, "lckv", [33, N], BF16))
        krr = E(_sbt(nc, "lkr", [32, N], BF16))
        wdq = [E(_sbt(nc, f"lwdq{i}", [128, 96], BF16)) for i in range(DC)]
        wuq = E(_sbt(nc, "lwuq", [32, 1024], BF16))
        wuk = E(_sbt(nc, "lwuk", [32, 1024], BF16))
        wuv = E(_sbt(nc, "lwuv", [33, 1024], BF16))
        wqr = E(_sbt(nc, "lwqr", [32, 256], BF16))
        bdq = E(_sbt(nc, "lbdq", [32, 2], F32))
        bkr = E(_sbt(nc, "lbkr", [32, 1], F32))
        buq = E(_sbt(nc, "lbuq", [128, 8], F32))
        buk = E(_sbt(nc, "lbuk", [128, 8], F32))
        bqr = E(_sbt(nc, "lbqr", [64, 4], F32))
        load_slab(c, "l1a_wdq", wdq, 0, D, 0, 32)
        load_slab(c, "l1a_wdkv", wdq, 0, D, 0, 32, dcol0=32)
        load_slab(c, "l1a_wkr", wdq, 0, D, 0, 32, dcol0=64)
        load_cast(c, "l1a_wuq", wuq[0:32, :])
        load_cast(c, "l1a_wuk", wuk[0:32, :])
        load_cast(c, "l1a_wuv_aug", wuv[0:33, :])
        tok = load_cast(c, "l1a_wqr", wqr[0:32, :])
        load_small(c, "l1a_bdq_col", bdq[0:32, 0:1])
        load_small(c, "l1a_bdkv_col", bdq[0:32, 1:2])
        load_small(c, "l1a_bkr_col", bkr[0:32, 0:1])
        load_small(c, "l1a_buq_col", buq[:])
        load_small(c, "l1a_buk_col", buk[:])
        tok2 = load_small(c, "l1a_bqr_col", bqr[0:64, :])
        p.dwait(tok)
        p.dwait(tok2)
        p.v(lambda e: e.memset(cq[32:33, :], 1.0))
        p.v(lambda e: e.memset(ckv[32:33, :], 1.0))
        p.barall()
        with scope(c) as cses:
            cps = [cses.enter_context(_pst(nc, f"lcp{i}", [32, 512], F32))
                   for i in range(2)]
            done = {}
            specs = [(0, cq, bdq[0:32, 0:1]), (32, ckv, bdq[0:32, 1:2]),
                     (64, krr, bkr[0:32, 0:1])]
            for jj, (col0, dst, bias) in enumerate(specs):
                for nb in range(4):
                    j = jj * 4 + nb
                    b = j % 2
                    sl = slice(512 * nb, 512 * (nb + 1))
                    ents = []
                    for i in range(DC):
                        ents.append((lambda e, st, sp, i=i, col0=col0, sl=sl, b=b:
                                     e.matmul(cps[b][:], wdq[i][:, col0:col0 + 32],
                                              c.yT[i][:, sl], start=st, stop=sp),
                                     [done.get(j - 2)] if i == 0 else None))
                    last = p.mmgroup(ents)
                    done[j] = p.v(lambda e, sl=sl, b=b, dst=dst, bias=bias:
                                  e.tensor_scalar(dst[0:32, sl], cps[b][:], bias, None,
                                                  ALU.add), wait=[last])
        p.barall()
        with scope(c) as gses:
            E2 = gses.enter_context
            qc = [E2(_sbt(nc, f"lqc{i}", [128, N], BF16)) for i in range(HG)]
            kc = [E2(_sbt(nc, f"lkc{i}", [128, N], BF16)) for i in range(HG)]
            ao = [E2(_sbt(nc, f"lao{i}", [128, N], BF16)) for i in range(HG)]
            qrt = [E2(_sbt(nc, f"lqr{i}", [32, N], BF16)) for i in range(2)]
            va = [E2(_sbt(nc, f"lva{i}", [128, 256], BF16)) for i in range(NT)]
            pt = [E2(_sbt(nc, f"lpt{i}", [128, 1024], BF16)) for i in range(2)]
            for g in range(4):
                with scope(c) as ps_es:
                    pps = [ps_es.enter_context(_pst(nc, f"lqp{i}", [128, 512], F32))
                           for i in range(2)]
                    done = {}
                    for jj, (wsb, dst, bias, sc) in enumerate(
                            ((wuq, qc, buq, SC), (wuk, kc, buk, 1.0))):
                        for j2 in range(HG * 4):
                            oc, nb = j2 // 4, j2 % 4
                            j = jj * 8 + j2
                            b = j % 2
                            sl = slice(512 * nb, 512 * (nb + 1))
                            col = 256 * g + 128 * oc
                            src = cq if jj == 0 else ckv
                            last = p.t(lambda e, b=b, sl=sl, col=col, wsb=wsb, src=src:
                                       e.matmul(pps[b][:], wsb[0:32, col:col + 128],
                                                src[0:32, sl], start=True, stop=True),
                                       wait=[done.get(j - 2)])
                            colj = (256 * g) // 128 + oc
                            done[j] = p.v(lambda e, oc=oc, sl=sl, b=b, colj=colj,
                                          dst=dst, bias=bias, sc=sc: e.tensor_scalar(
                                dst[oc][:, sl], pps[b][:], bias[:, colj:colj + 1], sc,
                                ALU.add, ALU.mult), wait=[last])
                    for nb in range(4):
                        b = nb % 2
                        sl = slice(512 * nb, 512 * (nb + 1))
                        last = p.t(lambda e, sl=sl, g=g, b=b: e.matmul(
                            pps[b][0:64, :], wqr[0:32, 64 * g:64 * (g + 1)],
                            cq[0:32, sl], start=True, stop=True),
                            wait=[done.get(14 + nb)])
                        for h2 in range(2):
                            done[16 + nb] = p.v(
                                lambda e, sl=sl, g=g, b=b, h2=h2: e.tensor_scalar(
                                    qrt[h2][0:32, sl], pps[b][32 * h2:32 * h2 + 32, :],
                                    bqr[32 * h2:32 * h2 + 32, g:g + 1], SC,
                                    ALU.add, ALU.mult), wait=[last])
                    vdone = {}
                    for nt in range(NT):
                        b = nt % 2
                        last = p.t(lambda e, nt=nt, g=g, b=b: e.matmul(
                            pps[b][:, 0:256], ckv[0:33, 128 * nt:128 * (nt + 1)],
                            wuv[0:33, 256 * g:256 * (g + 1)], start=True, stop=True),
                            wait=[done.get(18 + nt), vdone.get(nt - 2)])
                        vdone[nt] = p.v(lambda e, nt=nt, b=b: e.tensor_copy(
                            va[nt][:], pps[b][:, 0:256]), wait=[last])
                        done[18 + nt] = vdone[nt]
                p.barall()
                emit_negB(c, qc + qrt, kc + [krr], 160)
                with scope(c) as hses:
                    sps = [hses.enter_context(_pst(nc, f"lsp{i}", [128, 1024], F32))
                           for i in range(2)]
                    ops = hses.enter_context(_pst(nc, "lop", [128, 1024], F32))
                    sm1 = hses.enter_context(_pst(nc, "lsm", [1, 1024], F32))
                    expd = {}
                    prev_ml = None
                    for h2 in range(HG):
                        ro = 32 * h2
                        for qb in range(2):
                            q0 = 1024 * qb
                            qsl = slice(q0, q0 + 1024)
                            avd = {}
                            for kt in range(NT):
                                b = kt % 2
                                mm = None
                                for hf in range(2):
                                    p.t(lambda e, kt=kt, b=b, h2=h2, q0=q0, hf=hf:
                                        e.matmul(
                                        sps[b][:, 512 * hf:512 * (hf + 1)],
                                        kc[h2][:, 128 * kt:128 * (kt + 1)],
                                        qc[h2][:, q0 + 512 * hf:q0 + 512 * (hf + 1)],
                                        start=True, stop=False),
                                        wait=([expd.get(kt - 2)] if hf == 0 else None))
                                    mm = p.t(lambda e, kt=kt, b=b, h2=h2, q0=q0, hf=hf:
                                             e.matmul(
                                        sps[b][:, 512 * hf:512 * (hf + 1)],
                                        krr[0:32, 128 * kt:128 * (kt + 1)],
                                        qrt[h2][0:32,
                                                q0 + 512 * hf:q0 + 512 * (hf + 1)],
                                        start=False, stop=True))
                                ex = p.s(lambda e, b=b: e.activation(
                                    pt[b][:], sps[b][:], AF.Exp, bias=c.negb[:, 0:1]),
                                    wait=[mm, avd.get(kt - 2)])
                                expd[kt] = ex
                                av = None
                                for hf in range(2):
                                    p.t(lambda e, kt=kt, b=b, h2=h2, hf=hf: e.matmul(
                                        ops[:, 512 * hf:512 * (hf + 1)],
                                        va[kt][:, 128 * h2:128 * (h2 + 1)],
                                        pt[b][:, 512 * hf:512 * (hf + 1)],
                                        start=(kt == 0), stop=(kt == NT - 1)),
                                        wait=([ex, prev_ml] if (kt == 0 and hf == 0)
                                              else [ex]))
                                    av = p.t(lambda e, kt=kt, b=b, hf=hf: e.matmul(
                                        sm1[:, 512 * hf:512 * (hf + 1)],
                                        c.ones_col_bf[:],
                                        pt[b][:, 512 * hf:512 * (hf + 1)],
                                        start=(kt == 0), stop=(kt == NT - 1)))
                                avd[kt] = av
                            rc = p.v(lambda e: e.reciprocal(c.row1[0:1, 0:1024],
                                                            sm1[:]),
                                     wait=[avd[NT - 1]])
                            rp = None
                            for hf in range(2):
                                rp = p.t(lambda e, hf=hf: e.matmul(
                                    sps[0][0:128, 512 * hf:512 * (hf + 1)],
                                    c.ones_row[0:1, 0:128],
                                    c.row1[0:1, 512 * hf:512 * (hf + 1)],
                                    start=True, stop=True),
                                    wait=([rc, expd.get(NT - 2)] if hf == 0 else None))
                            cp = p.v(lambda e: e.tensor_copy(c.srec[:, 0:1024],
                                                             sps[0][:]),
                                     wait=[rp, prev_ml])
                            ml = p.v(lambda e, h2=h2, qsl=qsl: e.tensor_mul(
                                ao[h2][:, qsl], ops[:], c.srec[:, 0:1024]),
                                wait=[cp])
                            prev_ml = ml
                            expd = {-2: ml, -1: expd.get(NT - 1)}
                p.barall()
                emit_outproj(c, "l1a_wo", ao, 256 * g, 256,
                             with_bias_row=("l1a_bo_row" if g == 0 else None))


def emit_smoe(c):
    p, nc = c.p, c.nc
    with scope(c) as ses:
        E = ses.enter_context
        gcol = E(_sbt(nc, "sgc", [128, 8], F32))
        bcol = E(_sbt(nc, "sbc", [128, 8], F32))
        sscal = E(_sbt(nc, "ssc", [1, 1], F32))
        tok = load_small(c, "sm_gamma_col", gcol[:])
        tok = load_small(c, "sm_beta_col", bcol[:])
        tok = load_small(c, "sm_scale", sscal[:])
        p.dwait(tok)
        p.barall()
        emit_ln2(c, c.yT, c.yT, gcol=gcol, bcol=bcol)
        musb = [E(_sbt(nc, f"smu{i}", [128, 256], BF16)) for i in range(DC)]
        rmu = E(_sbt(nc, "srmu", [128, 2], F32))
        logT = [E(_sbt(nc, f"slg{i}", [128, N], BF16)) for i in range(2)]
        combT = [E(_sbt(nc, f"scb{i}", [128, N], BF16)) for i in range(2)]
        dtok = [E(_sbt(nc, f"sdt{i}", [128, 256], BF16)) for i in range(NT)]
        siT = [E(_sbt(nc, f"ssi{i}", [128, 256], BF16)) for i in range(DC)]
        hT = E(_sbt(nc, "shT", [128, 512], BF16))
        SO = [E(_sbt(nc, f"sso{i}", [128, 1024], BF16)) for i in range(2)]
        mcol = E(_sbt(nc, "smc", [128, 1], F32))
        scol = E(_sbt(nc, "ssm", [128, 1], F32))
        srow1 = E(_sbt(nc, "srow1", [1, 2048], F32))
        srow2 = E(_sbt(nc, "srow2", [1, 2048], F32))
        tok = load_slab(c, "sm_mu", musb, 0, D, 0, 256)
        p.dwait(tok)
        p.barall()
        pre_es = ses.enter_context(ExitStack())
        rnrep = pre_es.enter_context(_sbt(nc, "srn", [128, N], F32))
        dispP = [pre_es.enter_context(_sbt(nc, f"sdp{i}", [128, N], BF16))
                 for i in range(2)]
        sqscr = [pre_es.enter_context(_sbt(nc, f"ssq{i}", [128, N], BF16))
                 for i in range(2)]
        with scope(c) as nses:
            psn = nses.enter_context(_pst(nc, "spn", [1, N], F32))
            mm_done = {}
            for i in range(DC):
                b = i % 2
                sq = p.s(lambda e, i=i, b=b: e.activation(sqscr[b][:], c.yT[i][:],
                                                          AF.Square),
                         wait=[mm_done.get(i - 2)])
                last = None
                for nb in range(4):
                    sl = slice(512 * nb, 512 * (nb + 1))
                    last = p.t(lambda e, b=b, sl=sl, i=i: e.matmul(
                        psn[:, sl], c.ones_col_bf[:], sqscr[b][:, sl],
                        start=(i == 0), stop=(i == DC - 1)), wait=[sq])
                mm_done[i] = last
            p.bar()
            p.s(lambda e: e.activation(srow1[:], psn[:], AF.Ln))
            p.bar(("scalar",))
            p.s(lambda e: e.activation(srow1[:], srow1[:], AF.Exp, scale=0.5))
            p.bar(("vector", "scalar"))
            p.v(lambda e: e.tensor_scalar_max(srow1[:], srow1[:], 1e-12))
            p.bar(("vector",))
            p.v(lambda e: e.reciprocal(srow2[:], srow1[:]))
            p.bar(("vector",))
            p.v(lambda e: e.tensor_scalar_mul(srow2[:], srow2[:], sscal[0:1, 0:1]))
            p.bar()
        p.barall()
        with scope(c) as nses:
            psr = nses.enter_context(_pst(nc, "spr", [128, N], F32))
            for nb in range(4):
                sl = slice(512 * nb, 512 * (nb + 1))
                p.t(lambda e, sl=sl: e.matmul(psr[:, sl], c.ones_row[0:1, 0:128],
                                              srow2[:, sl], start=True, stop=True))
            p.bar(("tensor", "vector"))
            p.v(lambda e: e.tensor_copy(rnrep[:], psr[:]))
            p.bar(("tensor", "vector"))
        p.barall()
        with scope(c) as mses:
            psm = mses.enter_context(_pst(nc, "spm", [1, 256], F32))
            ptm = mses.enter_context(_pst(nc, "ptm", [128, 4], BF16))
            mm_done = {}
            for i in range(DC):
                b = i % 2
                sq = p.s(lambda e, i=i, b=b: e.activation(sqscr[b][:, 0:256],
                                                          musb[i][:], AF.Square),
                         wait=[mm_done.get(i - 2)])
                mm_done[i] = p.t(lambda e, b=b, i=i: e.matmul(
                    psm[:], c.ones_col_bf[:], sqscr[b][:, 0:256],
                    start=(i == 0), stop=(i == DC - 1)), wait=[sq])
            p.bar()
            p.s(lambda e: e.activation(srow1[:, 0:256], psm[:], AF.Ln))
            p.bar(("scalar",))
            p.s(lambda e: e.activation(srow1[:, 0:256], srow1[:, 0:256], AF.Exp,
                                       scale=0.5))
            p.bar(("vector", "scalar"))
            p.v(lambda e: e.tensor_scalar_max(srow1[:, 0:256], srow1[:, 0:256], 1e-12))
            p.bar(("vector",))
            p.v(lambda e: e.reciprocal(srow2[:, 0:256], srow1[:, 0:256]))
            p.bar(("vector", "scalar"))
            p.v(lambda e: e.tensor_copy(sqscr[0][0:1, 0:256], srow2[:, 0:256]))
            p.bar(("tensor", "vector"))
            for a in range(2):
                p.t(lambda e, a=a: e.transpose(
                    ptm[:, 2 * a:2 * a + 1], sqscr[0][0:1, 128 * a:128 * (a + 1)],
                    c.ident[0:1, 0:1]))
            p.bar(("tensor", "vector"))
            for a in range(2):
                p.v(lambda e, a=a: e.tensor_copy(rmu[:, a:a + 1], ptm[:, 2 * a:2 * a + 1]))
            p.bar(("tensor", "vector"))
        p.barall()
        with scope(c) as lses:
            psl = [lses.enter_context(_pst(nc, f"spl{i}", [128, 512], F32))
                   for i in range(2)]
            done = {}
            for j in range(2 * 4):
                a, nb = j // 4, j % 4
                b = j % 2
                sl = slice(512 * nb, 512 * (nb + 1))
                ents = []
                for i in range(DC):
                    ents.append((lambda e, st, sp, i=i, a=a, sl=sl, b=b: e.matmul(
                        psl[b][:], musb[i][:, 128 * a:128 * (a + 1)], c.yT[i][:, sl],
                        start=st, stop=sp), [done.get(j - 2)] if i == 0 else None))
                last = p.mmgroup(ents)
                done[j] = p.v(lambda e, a=a, sl=sl, b=b: e.scalar_tensor_tensor(
                    logT[a][:, sl], psl[b][:], rmu[:, a:a + 1], rnrep[:, sl],
                    ALU.mult, ALU.mult), wait=[last])
        p.barall()
        for a in range(2):
            p.v(lambda e, a=a: e.tensor_reduce(mcol[:], logT[a][:], AX.X, ALU.max))
            p.bar(("vector",))
            p.v(lambda e: e.tensor_scalar_mul(mcol[:], mcol[:], -1.0))
            p.bar(("vector", "scalar"))
            p.s(lambda e, a=a: e.activation(dispP[a][:], logT[a][:], AF.Exp,
                                            bias=mcol[:, 0:1], accum_out=scol[:]))
            p.bar(("vector", "scalar"))
            p.v(lambda e: e.reciprocal(scol[:], scol[:]))
            p.bar(("vector",))
            p.v(lambda e, a=a: e.tensor_scalar_mul(dispP[a][:], dispP[a][:],
                                                   scol[:, 0:1]))
            p.bar(("vector", "scalar"))
        p.barall()
        with scope(c) as tses:
            ptt = [tses.enter_context(_pst(nc, f"ptt{i}", [128, 128], BF16))
                   for i in range(2)]
            ctok = tses.enter_context(_sbt(nc, "sct", [128, 256], F32))
            ctokb = tses.enter_context(_sbt(nc, "sctb", [128, 256], BF16))
            cpd = {}
            jj = 0
            prev_exp = prev_tr3 = None
            for nt in range(NT):
                for a in range(2):
                    b = jj % 2
                    tr = p.t(lambda e, a=a, nt=nt, b=b: e.transpose(
                        ptt[b][:], dispP[a][:, 128 * nt:128 * (nt + 1)], c.ident[:]),
                        wait=[cpd.get(jj - 2)])
                    cpd[jj] = p.v(lambda e, a=a, nt=nt, b=b: e.tensor_copy(
                        dtok[nt][:, 128 * a:128 * (a + 1)], ptt[b][:]), wait=[tr])
                    jj += 1
                cts = []
                for a in range(2):
                    b = jj % 2
                    tr = p.t(lambda e, a=a, nt=nt, b=b: e.transpose(
                        ptt[b][:], logT[a][:, 128 * nt:128 * (nt + 1)], c.ident[:]),
                        wait=[cpd.get(jj - 2), prev_exp])
                    cpd[jj] = p.v(lambda e, a=a, b=b: e.tensor_copy(
                        ctok[:, 128 * a:128 * (a + 1)], ptt[b][:]), wait=[tr])
                    cts.append(cpd[jj])
                    jj += 1
                p.v(lambda e: e.tensor_reduce(mcol[:], ctok[:], AX.X, ALU.max),
                    wait=[prev_exp])
                ng = p.v(lambda e: e.tensor_scalar_mul(mcol[:], mcol[:], -1.0))
                ex = p.s(lambda e: e.activation(ctokb[:], ctok[:], AF.Exp,
                                                bias=mcol[:, 0:1], accum_out=scol[:]),
                         wait=[ng, cts[-1], prev_tr3])
                prev_exp = ex
                p.v(lambda e: e.reciprocal(scol[:], scol[:]), wait=[ex])
                ml = p.v(lambda e: e.tensor_scalar_mul(ctokb[:], ctokb[:],
                                                       scol[:, 0:1]))
                for a in range(2):
                    b = jj % 2
                    tr = p.t(lambda e, a=a, b=b: e.transpose(
                        ptt[b][:], ctokb[:, 128 * a:128 * (a + 1)], c.ident[:]),
                        wait=[ml, cpd.get(jj - 2)])
                    prev_tr3 = tr
                    cpd[jj] = p.v(lambda e, a=a, nt=nt, b=b: e.tensor_copy(
                        combT[a][:, 128 * nt:128 * (nt + 1)], ptt[b][:]), wait=[tr])
                    jj += 1
        p.barall()
        p.seal()
        pre_es.close()
        with scope(c) as sses:
            psi = [sses.enter_context(_pst(nc, f"psi{i}", [128, 256], F32))
                   for i in range(2)]
            ptx = [sses.enter_context(_pst(nc, f"ptx{i}", [128, 128], BF16))
                   for i in range(2)]
            xnt = [sses.enter_context(_sbt(nc, f"sxn{i}", [128, 1024], BF16))
                   for i in range(NT)]
            cpd = {}
            for j in range(NT * DC):
                nt, dc = j // DC, j % DC
                b = j % 2
                tr = p.t(lambda e, dc=dc, nt=nt, b=b: e.transpose(
                    ptx[b][:], c.yT[dc][:, 128 * nt:128 * (nt + 1)], c.ident[:]),
                    wait=[cpd.get(j - 2)])
                cpd[j] = p.v(lambda e, dc=dc, nt=nt, b=b: e.tensor_copy(
                    xnt[nt][:, 128 * dc:128 * (dc + 1)], ptx[b][:]), wait=[tr])
            p.barall()
            done = {}
            for dc in range(DC):
                b = dc % 2
                ents = []
                for nt in range(NT):
                    ents.append((lambda e, st, sp, dc=dc, nt=nt, b=b: e.matmul(
                        psi[b][:], xnt[nt][:, 128 * dc:128 * (dc + 1)],
                        dtok[nt][:, 0:256], start=st, stop=sp),
                        [done.get(dc - 2)] if nt == 0 else None))
                last = p.mmgroup(ents)
                done[dc] = p.v(lambda e, dc=dc, b=b: e.tensor_copy(
                    siT[dc][:], psi[b][:]), wait=[last])
        p.barall()
        with scope(c) as eses:
            E3 = eses.enter_context
            w1sl = [E3(_sbt(nc, f"sw1{i}", [128, 256], BF16)) for i in range(DC)]
            w2sl = [E3(_sbt(nc, f"sw2{i}", [128, 1024], BF16)) for i in range(2)]
            b1row = E3(_sbt(nc, "sb1", [1, 256], F32))
            b2row = E3(_sbt(nc, "sb2r", [1, 1024], F32))
            hgbf = E3(_sbt(nc, "shg", [16, 256], BF16))
            psh = E3(_pst(nc, "psh", [16, 256], F32))
            pth = E3(_pst(nc, "pth", [128, 16], BF16))
            pso = [E3(_pst(nc, f"pso{i}", [128, 512], F32)) for i in range(4)]
            ptxx = E3(_pst(nc, "ptxx", [128, 128], BF16))
            soT = [E3(_sbt(nc, f"ssoT{i}", [128, 256], BF16)) for i in range(DC)]
            for ex in range(16):
                p.barall()
                load_slab(c, "sm_w1", w1sl, 0, D, 0, 256, idx0=ex)
                load_slab(c, "sm_w2", w2sl, 0, 256, 0, 1024, idx0=ex)
                tok = c.p.dma(lambda e, ex=ex: e.dma_start(out=b1row[:],
                                                           in_=c.hbm["sm_b1"][ex]))
                tok2 = c.p.dma(lambda e, ex=ex: e.dma_start(out=b2row[:],
                                                            in_=c.hbm["sm_b2"][ex]))
                p.dwait(tok2)
                p.barall()
                ents = []
                for dc in range(DC):
                    ents.append((lambda e, st, sp, dc=dc, ex=ex: e.matmul(
                        psh[:], siT[dc][:, 16 * ex:16 * (ex + 1)], w1sl[dc][:, 0:256],
                        start=st, stop=sp), None))
                ents.append((lambda e, st, sp: e.matmul(
                    psh[:], c.ones_row[0:1, 0:16], b1row[0:1, :],
                    start=st, stop=sp), None))
                last = p.mmgroup(ents)
                gl = p.s(lambda e: e.activation(hgbf[:], psh[:], AF.Gelu), wait=[last])
                p.bar(("tensor", "vector", "scalar"))
                for a in range(2):
                    tr = p.t(lambda e, a=a: e.transpose(
                        pth[:], hgbf[0:16, 128 * a:128 * (a + 1)], c.ident[0:16, 0:16]))
                    p.v(lambda e, a=a: e.tensor_copy(
                        hT[:, 16 * a:16 * a + 16], pth[:, 0:16]), wait=[tr])
                    p.bar(("tensor", "vector"))
                for dc in range(DC):
                    ents = []
                    for a in range(2):
                        ents.append((lambda e, st, sp, a=a, dc=dc, ex=ex: e.matmul(
                            pso[dc // 2][:, 256 * (dc % 2) + 16 * ex:
                                         256 * (dc % 2) + 16 * (ex + 1)],
                            w2sl[a][:, 128 * dc:128 * (dc + 1)],
                            hT[:, 16 * a:16 * a + 16], start=st, stop=sp), None))
                    ents.append((lambda e, st, sp, dc=dc, ex=ex: e.matmul(
                        pso[dc // 2][:, 256 * (dc % 2) + 16 * ex:
                                     256 * (dc % 2) + 16 * (ex + 1)],
                        b2row[0:1, 128 * dc:128 * (dc + 1)], c.ones_row[0:1, 0:16],
                        start=st, stop=sp), None))
                    p.mmgroup(ents)
                p.bar(("tensor", "vector", "scalar"))
            p.barall()
            for dc in range(DC):
                p.v(lambda e, dc=dc: e.tensor_copy(
                    soT[dc][:], pso[dc // 2][:, 256 * (dc % 2):256 * (dc % 2) + 256]))
            p.bar(("tensor", "vector"))
            for dc in range(DC):
                for a in range(2):
                    tr = p.t(lambda e, dc=dc, a=a: e.transpose(
                        ptxx[:], soT[dc][:, 128 * a:128 * (a + 1)], c.ident[:]))
                    p.v(lambda e, dc=dc, a=a: e.tensor_copy(
                        SO[a][:, 128 * dc:128 * (dc + 1)], ptxx[:]), wait=[tr])
                    p.bar(("tensor", "vector"))
        p.barall()
        with scope(c) as oses:
            pps = [oses.enter_context(_pst(nc, f"sop{i}", [128, 512], F32))
                   for i in range(2)]
            done = {}
            for j in range(DC * 4):
                dc, nb = j // 4, j % 4
                b = j % 2
                sl = slice(512 * nb, 512 * (nb + 1))
                ents = []
                for a in range(2):
                    ents.append((lambda e, st, sp, a=a, dc=dc, sl=sl, b=b: e.matmul(
                        pps[b][:], SO[a][:, 128 * dc:128 * (dc + 1)], combT[a][:, sl],
                        start=st, stop=sp), [done.get(j - 2)] if a == 0 else None))
                last = p.mmgroup(ents)
                done[j] = p.v(lambda e, dc=dc, sl=sl, b=b: e.tensor_add(
                    c.xT[dc][:, sl], c.xT[dc][:, sl], pps[b][:]), wait=[last])
        p.barall()


L0A_CFG = dict(nh=16, scale=1.0 / 32.0, causal=False,
               wq="l0a_wqkvT", wk="l0a_wqkvT", wv="l0a_wqkvT",
               qoff=0, koff=1024, voff=2048,
               wo="l0a_woutT", bout_row="l0a_bout_row", add_y=True)
L1B_CFG = dict(nh=16, scale=1.0 / 32.0, causal=True,
               wq="l1b_wqT", wk="l1b_wkT", wv="l1b_wvT",
               qoff=0, koff=0, voff=0,
               bq_col="l1b_bq_col", bk_col="l1b_bk_col", bv_row="l1b_bv_row",
               wo="l1b_woT", bout_row="l1b_bo_row", add_y=False)


def emit_step(c, stp):
    if stp == "l0a":
        emit_ln2(c, c.xT, c.yT)
        emit_attn_std(c, L0A_CFG)
    elif stp == "moe":
        emit_ln2(c, c.xT, c.yT)
        emit_moe(c)
    elif stp == "l0b":
        emit_ln2(c, c.xT, c.yT)
        emit_l0b(c)
    elif stp == "l1a":
        emit_ln2(c, c.xT, c.yT)
        emit_l1a(c)
    elif stp == "smoe":
        emit_ln2(c, c.xT, c.yT)
        emit_smoe(c)
    elif stp == "l1b":
        emit_ln2(c, c.xT, c.yT)
        emit_attn_std(c, L1B_CFG)
    elif stp == "lntest":
        emit_ln2(c, c.xT, c.yT)
        for i in range(DC):
            c.p.v(lambda e, i=i: e.tensor_copy(c.xT[i][:], c.yT[i][:]))
        c.p.barall()
        c.p.seal()
    else:
        raise ValueError(stp)


def build_program(W, steps):
    nc = bass.Bass("TRN2", target_bir_lowering=False)
    with ExitStack() as es:
        p = P(nc, es)
        c = make_ctx(nc, es, p, W, [("xin", [D, N], "ExternalInput"),
                                    ("xout", [D, N], "ExternalOutput")])
        p.blk = es.enter_context(nc.Block())
        load_consts(c)
        tok = None
        for i in range(DC):
            tok = p.dma(lambda e, i=i: e.dma_start(
                out=c.xT[i][:], in_=c.hbm["xin"][128 * i:128 * (i + 1), :]))
        p.dwait(tok)
        p.barall()
        p.seal()
        for stp in steps:
            emit_step(c, stp)
        tok = None
        for i in range(DC):
            tok = p.dma(lambda e, i=i: e.dma_start(
                out=c.hbm["xout"][128 * i:128 * (i + 1), :], in_=c.xT[i][:]),
                wait=[("dve", p.cnt.get("dve", 0)), ("act", p.cnt.get("act", 0))])
        p.dwait(tok)
        p.seal()
    return nc


def kernel(**inputs):
    _ = (inputs["L0_attn_b_v_lod1"], inputs["L0_attn_b_k_lod2"],
         inputs["L0_attn_b_v_lod2"])
    W = prep_weights(inputs)
    x = np.ascontiguousarray(np.asarray(inputs["L0_attn_a_x"], dtype=np.float32))
    nc = build_program(W, list(ALL_STEPS))
    in_maps = []
    for ci in range(8):
        m = dict(W)
        m["xin"] = np.ascontiguousarray(x[ci].T)
        in_maps.append(m)
    res = run_bass_kernel_spmd(nc, in_maps, core_ids=list(range(8)))
    return np.ascontiguousarray(
        np.stack([np.asarray(res.results[ci]["xout"]).T for ci in range(8)],
                 0)).astype(np.float32)


# version 08 (final)
# speedup vs baseline: 1.3483x; 1.0157x over previous
import numpy as np
import ml_dtypes
from contextlib import ExitStack
import concourse.bass as bass
import concourse.mybir as mybir
from concourse.bass_utils import run_bass_kernel_spmd

F32 = mybir.dt.float32
BF16 = mybir.dt.bfloat16
AF = mybir.ActivationFunctionType
ALU = mybir.AluOpType
AX = mybir.AxisListType
BF = ml_dtypes.bfloat16

D = 1024
N = 2048
NT = N // 128
DC = D // 128
NBIG = 80.0
BAND_OFF = 2048
STEP_OFF = 2432
ENGINES = ("tensor", "vector", "scalar", "gpsimd", "sync")
ALL_STEPS = ("l0a", "moe", "l0b", "moe", "l1a", "smoe", "l1b", "smoe")


def _f(a):
    return np.ascontiguousarray(np.asarray(a), dtype=np.float32)


def prep_weights(I):
    W = {}
    wqkv = np.asarray(I["L0_attn_a_W_qkv"])
    wr = wqkv.reshape(16, 3, 64, 1024)
    W["l0a_wqkvT"] = _f(np.concatenate([wr[:, 0].reshape(1024, 1024),
                                        wr[:, 1].reshape(1024, 1024),
                                        wr[:, 2].reshape(1024, 1024)], 0).T)
    W["l0a_woutT"] = _f(np.asarray(I["L0_attn_a_W_out"]).T)
    W["l0a_bout_row"] = _f(np.asarray(I["L0_attn_a_b_out"]).reshape(1, 1024))

    w12 = np.asarray(I["L0_ffn_w12"])
    w3 = np.asarray(I["L0_ffn_w3"])
    sw1 = np.asarray(I["L0_ffn_shared_w1"])
    sw2 = np.asarray(I["L0_ffn_shared_w2"])
    sw3 = np.asarray(I["L0_ffn_shared_w3"])
    w12T = [np.ascontiguousarray(w12[e].T) for e in range(8)]
    w12T += [np.ascontiguousarray(np.concatenate(
        [sw1[1024 * u:1024 * (u + 1)], sw2[1024 * u:1024 * (u + 1)]], 0).T)
        for u in range(4)]
    w12Ti = [w.reshape(1024, 2, 8, 128).transpose(0, 2, 1, 3).reshape(1024, 2048)
             for w in w12T]
    W["moe_w12T"] = _f(np.stack(w12Ti))
    w3T = [np.ascontiguousarray(w3[e].T) for e in range(8)]
    w3T += [np.ascontiguousarray(sw3[:, 1024 * u:1024 * (u + 1)].T) for u in range(4)]
    W["moe_w3T"] = _f(np.stack(w3T))
    W["moe_routerT"] = _f(np.asarray(I["L0_ffn_router_w"]).T)
    W["moe_ebias_row"] = _f(np.asarray(I["L0_ffn_expert_bias"]).reshape(1, 8))

    W["l0b_wqT"] = _f(np.asarray(I["L0_attn_b_W_q"]).T)
    W["l0b_wkT"] = _f(np.asarray(I["L0_attn_b_W_k"]).T)
    W["l0b_wvT"] = _f(np.asarray(I["L0_attn_b_W_v"]).T)
    W["l0b_woT"] = _f(np.asarray(I["L0_attn_b_W_o"]).T)
    kl1 = np.asarray(I["L0_attn_b_k_lod1"])
    kgt = kl1.reshape(64, 32, 64).transpose(2, 1, 0).reshape(64, 2048)
    W["l0b_klod1gT2"] = _f(np.concatenate([kgt, kgt], 0))

    W["l1a_wdq"] = _f(I["L1_attn_a_W_DQ"])
    W["l1a_bdq_col"] = _f(np.asarray(I["L1_attn_a_b_DQ"]).reshape(32, 1))
    W["l1a_wdkv"] = _f(I["L1_attn_a_W_DKV"])
    W["l1a_bdkv_col"] = _f(np.asarray(I["L1_attn_a_b_DKV"]).reshape(32, 1))
    W["l1a_wuq"] = _f(I["L1_attn_a_W_UQ"])
    W["l1a_buq_col"] = _f(np.asarray(I["L1_attn_a_b_UQ"]).reshape(8, 128).T)
    W["l1a_wuk"] = _f(I["L1_attn_a_W_UK"])
    W["l1a_buk_col"] = _f(np.asarray(I["L1_attn_a_b_UK"]).reshape(8, 128).T)
    W["l1a_wuv_aug"] = _f(np.concatenate(
        [np.asarray(I["L1_attn_a_W_UV"]),
         np.asarray(I["L1_attn_a_b_UV"]).reshape(1, 1024)], 0))
    W["l1a_wqr"] = _f(I["L1_attn_a_W_QR"])
    W["l1a_bqr_col"] = _f(np.asarray(I["L1_attn_a_b_QR"]).reshape(4, 64).T)
    W["l1a_wkr"] = _f(I["L1_attn_a_W_KR"])
    W["l1a_bkr_col"] = _f(np.asarray(I["L1_attn_a_b_KR"]).reshape(32, 1))
    W["l1a_wo"] = _f(I["L1_attn_a_W_O"])
    W["l1a_bo_row"] = _f(np.asarray(I["L1_attn_a_b_O"]).reshape(1, 1024))

    W["l1b_wqT"] = _f(np.asarray(I["L1_attn_b_Wq_w"]).T)
    W["l1b_bq_col"] = _f(np.asarray(I["L1_attn_b_Wq_b"]).reshape(8, 128).T)
    W["l1b_wkT"] = _f(np.asarray(I["L1_attn_b_Wk_w"]).T)
    W["l1b_bk_col"] = _f(np.asarray(I["L1_attn_b_Wk_b"]).reshape(8, 128).T)
    W["l1b_wvT"] = _f(np.asarray(I["L1_attn_b_Wv_w"]).T)
    W["l1b_bv_row"] = _f(np.asarray(I["L1_attn_b_Wv_b"]).reshape(1, 1024))
    W["l1b_woT"] = _f(np.asarray(I["L1_attn_b_Wo_w"]).T)
    W["l1b_bo_row"] = _f(np.asarray(I["L1_attn_b_Wo_b"]).reshape(1, 1024))

    W["sm_mu"] = _f(np.asarray(I["L1_ffn_mu"]).reshape(1024, 256))
    W["sm_gamma_col"] = _f(np.asarray(I["L1_ffn_ln_gamma"]).reshape(8, 128).T)
    W["sm_beta_col"] = _f(np.asarray(I["L1_ffn_ln_beta"]).reshape(8, 128).T)
    W["sm_scale"] = _f(np.asarray(I["L1_ffn_scale"]).reshape(1, 1))
    W["sm_w1"] = _f(I["L1_ffn_w1"])
    W["sm_b1"] = _f(np.asarray(I["L1_ffn_b1"]).reshape(16, 1, 256))
    W["sm_w2"] = _f(I["L1_ffn_w2"])
    W["sm_b2"] = _f(np.asarray(I["L1_ffn_b2"]).reshape(16, 1, 1024))

    W["ident_bf"] = np.eye(128, dtype=BF)
    sel8 = np.zeros((8, 1024), np.float32)
    for e in range(8):
        sel8[e, 128 * e:128 * (e + 1)] = 1.0
    W["sel8"] = sel8.astype(BF)
    W["ones_col_bf"] = np.ones((128, 1), dtype=BF)
    W["ones_row"] = np.ones((1, 2048), np.float32)
    W["ones_row_bf"] = np.ones((1, 128), dtype=BF)

    inv_freq = 1.0 / (10000.0 ** (np.arange(0, 64, 2, dtype=np.float64) / 64.0))
    freqs = np.arange(2048, dtype=np.float64)[:, None] * inv_freq[None, :]
    emb = np.concatenate([freqs, freqs], -1)
    cosT = np.cos(emb).T.astype(np.float32)
    sinT = np.sin(emb).T.astype(np.float32)
    W["rope_cos2"] = _f(np.concatenate([cosT, cosT], 0))
    W["rope_sin2"] = _f(np.concatenate([sinT, sinT], 0))
    pi64 = np.zeros((64, 64), np.float32)
    for i in range(32):
        pi64[i, 32 + i] = -1.0
        pi64[32 + i, i] = 1.0
    pi128 = np.zeros((128, 128), np.float32)
    pi128[:64, :64] = pi64
    pi128[64:, 64:] = pi64
    W["pi128T"] = pi128.T.astype(BF)
    t_idx = np.arange(64)[:, None]
    k_idx = np.arange(2048)[None, :]
    eb = (NBIG * (k_idx // 32 == t_idx)).astype(np.float32)
    W["ebig2"] = np.concatenate([eb, eb], 0).astype(BF)
    r = np.arange(128)[:, None]
    u = np.arange(4480)[None, :]
    dqk = u - BAND_OFF - r
    W["mask_band"] = ((dqk >= 0) & (dqk <= 511)).astype(BF)
    u2 = np.arange(4992)[None, :]
    W["mask_step"] = ((u2 - STEP_OFF - r) >= 0).astype(BF)
    return W


class P:
    def __init__(self, nc, es):
        self.nc = nc
        self.es = es
        self.q = {e: [] for e in ENGINES}
        self.sems = {}
        self.cnt = {}

    def sem(self, name):
        if name not in self.sems:
            self.sems[name] = self.es.enter_context(self.nc.semaphore(name))
            self.cnt[name] = 0
        return self.sems[name]

    CHAIN = {}

    def _emit(self, eng, fn, inc, wait):
        w = []
        ch = self.CHAIN.get(eng)
        if ch is not None and fn is not None and inc == ch:
            prev = self.cnt.get(ch, 0)
            if prev > 0:
                self.sem(ch)
                w.append((ch, prev))
        for it in (wait or []):
            if it is None:
                continue
            s, cc = it
            self.sem(s)
            w.append((s, cc))
        newc = None
        if inc:
            self.sem(inc)
            self.cnt[inc] += 16 if inc.startswith("D") else 1
            newc = self.cnt[inc]
        self.q[eng].append((fn, inc, w))
        return (inc, newc) if inc else None

    def t(self, fn, wait=None):
        return self._emit("tensor", fn, "pe", wait)

    def v(self, fn, wait=None):
        return self._emit("vector", fn, "dve", wait)

    def s(self, fn, wait=None):
        return self._emit("scalar", fn, "act", wait)

    def g(self, fn, wait=None):
        return self._emit("gpsimd", fn, "gp", wait)

    def dma(self, fn, wait=None, q="sync", sem=None):
        return self._emit(q, fn, sem or ("Dsy" if q == "sync" else "Dgp"), wait)

    def dwait(self, tok, eng=None):
        s, cc = tok
        eng = eng or ("sync" if s == "Dsy" else "gpsimd")
        self._emit(eng, None, None, [tok])

    def mmgroup(self, entries):
        last = None
        nn = len(entries)
        for i, (fn, w) in enumerate(entries):
            last = self.t(lambda e, fn=fn, i=i: fn(e, i == 0, i == nn - 1), wait=w)
        return last

    def bar(self, engines=("tensor", "vector", "scalar", "gpsimd")):
        waits = [(s, cc) for s, cc in self.cnt.items()
                 if cc > 0 and not s.startswith("bar_")]
        for s, _ in waits:
            self.sem(s)
        for en in engines:
            self.q[en].append((None, None, list(waits)))

    def barall(self):
        self.bar(ENGINES)

    def seal(self):
        self.flush(self.blk)

    def flush(self, blk):
        for en in ENGINES:
            items = self.q[en]
            if not items:
                continue

            def runner(e, items=items):
                for fn, inc, waits in items:
                    for s, cc in waits:
                        e.wait_ge(self.sems[s], cc)
                    if fn is not None:
                        ins = fn(e)
                        if inc is not None:
                            ins.then_inc(self.sems[inc],
                                         16 if inc.startswith("D") else 1)
            getattr(blk, en)(runner)
            self.q[en] = []


class Ctx:
    pass


_UNIQ = [0]


def _sbt(nc, name, shape, dt):
    _UNIQ[0] += 1
    return nc.sbuf_tensor(f"{name}_{_UNIQ[0]}", shape, dt)


def _pst(nc, name, shape, dt):
    _UNIQ[0] += 1
    return nc.psum_tensor(f"{name}_{_UNIQ[0]}", shape, dt)



def make_ctx(nc, es, p, W, io_names):
    c = Ctx()
    c.nc, c.es, c.p, c.W = nc, es, p, W
    c.hbm = {}
    for name, arr in W.items():
        dt = BF16 if arr.dtype == BF else F32
        c.hbm[name] = nc.dram_tensor(name, list(arr.shape), dt, kind="ExternalInput").ap()
    for name, shape, kind in io_names:
        c.hbm[name] = nc.dram_tensor(name, shape, F32, kind=kind).ap()
    sb = lambda nm, sh, dt=F32: es.enter_context(_sbt(nc, "S_" + nm, sh, dt))
    c.sb = sb
    c.xT = [sb(f"xT{i}", [128, N]) for i in range(DC)]
    c.yT = [sb(f"yT{i}", [128, N], BF16) for i in range(DC)]
    c.ident = sb("ident", [128, 128], BF16)
    c.ones_col_bf = sb("ones_col_bf", [128, 1], BF16)
    c.ones_row = sb("ones_row", [1, 2048])
    c.ones_row_bf = sb("ones_row_bf", [1, 128], BF16)
    c.row1 = sb("row1", [1, 2048])
    c.negb = sb("negb", [128, 2])
    c.srec = sb("srec", [128, 1024])
    c.epscol = sb("epscol", [1, 1])
    return c


def load_consts(c):
    p = c.p
    p.v(lambda e: e.memset(c.epscol[:], 1e-5))
    tok = None
    for nm, dst in (("ident_bf", c.ident), ("ones_col_bf", c.ones_col_bf),
                    ("ones_row", c.ones_row), ("ones_row_bf", c.ones_row_bf)):
        tok = p.dma(lambda e, nm=nm, dst=dst: e.dma_start(out=dst[:], in_=c.hbm[nm]))
    p.dwait(tok)
    p.barall()


def load_slab(c, hbm_name, dst_tiles, row0, nrows, col0, cols, idx0=None, dcol0=0):
    p = c.p
    tok = None
    for i in range(nrows // 128):
        src = c.hbm[hbm_name]
        if idx0 is not None:
            src = src[idx0]
        tok = p.dma(lambda e, i=i, src=src: e.dma_start(
            out=dst_tiles[i][:, dcol0:dcol0 + cols],
            in_=src[row0 + 128 * i:row0 + 128 * (i + 1), col0:col0 + cols]), q="gpsimd")
    return tok


def load_small(c, hbm_name, dst_ap):
    return c.p.dma(lambda e, dst_ap=dst_ap: e.dma_start(out=dst_ap, in_=c.hbm[hbm_name]))


def emit_ln2(c, src, out, gcol=None, bcol=None):
    p, nc = c.p, c.nc
    src_bf = (src is c.yT)
    with scope(c) as _oses:
        row2 = _oses.enter_context(_sbt(nc, "lnrow2", [1, 2048], F32))
        row3 = _oses.enter_context(_sbt(nc, "lnrow3", [1, 2048], F32))
        _emit_ln2_body(c, src, out, gcol, bcol, row2, row3)


def _emit_ln2_body(c, src, out, gcol, bcol, row2, row3):
    p, nc = c.p, c.nc
    src_bf = (src is c.yT)
    with scope(c) as ses:
        E = ses.enter_context
        mirc = [E(_sbt(nc, f"ln_m{i}", [128, N], BF16)) for i in range(2)]
        sqc = [E(_sbt(nc, f"ln_s{i}", [128, N], BF16)) for i in range(2)]
        repR = E(_sbt(nc, "ln_repR", [128, N], F32))
        repMR = E(_sbt(nc, "ln_repMR", [128, N], F32))
        tscr = E(_sbt(nc, "ln_tscr", [128, N], F32))
        uscr = E(_sbt(nc, "ln_uscr", [128, N], F32)) if gcol is not None else None
        ps_sum = E(_pst(nc, "ln_sum", [1, N], F32))
        ps_sq = E(_pst(nc, "ln_sq", [1, N], F32))
        mm_done = {}
        for i in range(DC):
            b = i % 2
            if src_bf:
                mt = src[i]
                cp = None
            else:
                mt = mirc[b]
                cp = p.s(lambda e, i=i, b=b: e.activation(mirc[b][:], src[i][:], AF.Copy),
                         wait=[mm_done.get(i - 2)])
            sq = p.s(lambda e, mt=mt, b=b: e.activation(sqc[b][:], mt[:], AF.Square),
                     wait=[mm_done.get(i - 2)] if src_bf else [cp])
            last = None
            for nb in range(4):
                sl = slice(512 * nb, 512 * (nb + 1))
                p.t(lambda e, mt=mt, sl=sl, i=i: e.matmul(
                    ps_sum[:, sl], c.ones_col_bf[:], mt[:, sl],
                    start=(i == 0), stop=(i == DC - 1)), wait=[cp])
                last = p.t(lambda e, b=b, sl=sl, i=i: e.matmul(
                    ps_sq[:, sl], c.ones_col_bf[:], sqc[b][:, sl],
                    start=(i == 0), stop=(i == DC - 1)), wait=[sq])
            mm_done[i] = last
        p.bar()
        p.v(lambda e: e.tensor_scalar_mul(c.row1[:], ps_sum[:], 1.0 / 1024.0))
        p.bar(("vector", "scalar"))
        p.s(lambda e: e.activation(row2[:], c.row1[:], AF.Square))
        p.bar(("vector", "scalar"))
        p.v(lambda e: e.scalar_tensor_tensor(row3[:], ps_sq[:], 1.0 / 1024.0, row2[:],
                                             ALU.mult, ALU.subtract))
        p.bar(("vector", "scalar"))
        p.s(lambda e: e.activation(row2[:], row3[:], AF.Ln, bias=c.epscol[0:1, 0:1]))
        p.bar(("scalar",))
        p.s(lambda e: e.activation(row2[:], row2[:], AF.Exp, scale=-0.5))
        p.bar(("vector", "scalar"))
        p.v(lambda e: e.tensor_mul(row3[:], c.row1[:], row2[:]))
        p.bar()
    with scope(c) as ses:
        E = ses.enter_context
        repR = E(_sbt(nc, "ln_repR", [128, N], F32))
        repMR = E(_sbt(nc, "ln_repMR", [128, N], F32))
        tscr = E(_sbt(nc, "ln_tscr", [128, N], F32))
        uscr = E(_sbt(nc, "ln_uscr", [128, N], F32)) if gcol is not None else None
        ps_rep = E(_pst(nc, "ln_rep", [128, N], F32))
        for row, dst in ((row2, repR), (row3, repMR)):
            for nb in range(4):
                sl = slice(512 * nb, 512 * (nb + 1))
                p.t(lambda e, sl=sl, row=row: e.matmul(
                    ps_rep[:, sl], c.ones_row[0:1, 0:128], row[:, sl],
                    start=True, stop=True))
            p.bar(("tensor", "vector"))
            p.v(lambda e, dst=dst: e.tensor_copy(dst[:], ps_rep[:]))
            p.bar(("tensor", "vector"))
        for i in range(DC):
            st = src[i]
            dv = p.v(lambda e, st=st: e.tensor_mul(tscr[:], st[:], repR[:]))
            if gcol is None:
                p.g(lambda e, i=i: e.tensor_sub(out[i][:], tscr[:], repMR[:]), wait=[dv])
                p.bar(("vector", "gpsimd"))
            else:
                gv = p.g(lambda e: e.tensor_sub(uscr[:], tscr[:], repMR[:]), wait=[dv])
                p.v(lambda e, i=i: e.tensor_scalar(out[i][:], uscr[:],
                                                   gcol[:, i:i + 1], bcol[:, i:i + 1],
                                                   ALU.mult, ALU.add), wait=[gv])
                p.bar(("vector", "gpsimd", "scalar"))
    p.barall()


from contextlib import contextmanager


@contextmanager
def scope(c):
    es = ExitStack()
    try:
        yield es
    finally:
        c.p.seal()
        es.close()


def emit_negB(c, qtiles, ktiles, dh_total):
    p, nc = c.p, c.nc
    with scope(c) as ses:
        E = ses.enter_context
        sqb = E(_sbt(nc, "nb_sq", [128, N], BF16))
        r1 = E(_sbt(nc, "nb_r1", [1, 2], F32))
        psn = E(_pst(nc, "nb_ps", [1, N], F32))
        pb = E(_pst(nc, "nb_pb", [128, 2], F32))
        lastmm = None
        for j, tiles in enumerate((qtiles, ktiles)):
            ntl = len(tiles)
            for t, tl in enumerate(tiles):
                pp = tl.shape[0]
                sq = p.s(lambda e, tl=tl, pp=pp: e.activation(sqb[0:pp, :], tl[:],
                                                              AF.Square), wait=[lastmm])
                for nb in range(4):
                    sl = slice(512 * nb, 512 * (nb + 1))
                    lastmm = p.t(lambda e, pp=pp, sl=sl, t=t, ntl=ntl: e.matmul(
                        psn[:, sl], c.ones_col_bf[0:pp, 0:1], sqb[0:pp, sl],
                        start=(t == 0), stop=(t == ntl - 1)), wait=[sq])
            p.bar(("tensor", "vector"))
            p.v(lambda e, j=j: e.tensor_reduce(r1[:, j:j + 1], psn[:], AX.X, ALU.max))
            p.bar(("tensor", "vector"))
        p.v(lambda e: e.tensor_mul(r1[:, 0:1], r1[:, 0:1], r1[:, 1:2]))
        p.bar(("vector", "scalar"))
        p.s(lambda e: e.activation(r1[:, 0:1], r1[:, 0:1], AF.Ln))
        p.bar(("vector", "scalar"))
        p.s(lambda e: e.activation(r1[:, 0:1], r1[:, 0:1], AF.Exp, scale=0.5))
        p.bar(("vector", "scalar"))
        p.v(lambda e: e.tensor_scalar(r1[:, 0:1], r1[:, 0:1], 2.0, None, ALU.add))
        p.bar(("vector",))
        p.v(lambda e: e.tensor_scalar(r1[:, 1:2], r1[:, 0:1], NBIG, None, ALU.add))
        p.bar(("tensor", "vector"))
        for j in range(2):
            p.t(lambda e, j=j: e.matmul(pb[:, j:j + 1], c.ones_row[0:1, 0:128],
                                        r1[:, j:j + 1], start=True, stop=True))
        p.bar(("tensor", "vector"))
        p.v(lambda e: e.tensor_scalar_mul(c.negb[:], pb[:], -1.0))
    p.barall()


def emit_attn_std(c, cfg):
    p, nc = c.p, c.nc
    nh = cfg["nh"]
    HG, gd = 4, 256
    with scope(c) as ses:
        E = ses.enter_context
        qa = [E(_sbt(nc, f"aq{i}", [128, N], BF16)) for i in range(2)]
        ka = [E(_sbt(nc, f"ak{i}", [128, N], BF16)) for i in range(2)]
        ao = [E(_sbt(nc, f"ao{i}", [128, N], BF16)) for i in range(2)]
        va = [E(_sbt(nc, f"avv{i}", [128, HG * 65], BF16)) for i in range(NT)]
        pt = [E(_sbt(nc, f"apt{i}", [128, 1024], BF16)) for i in range(2)]
        wsl = [E(_sbt(nc, f"awr{i}", [128, gd], BF16)) for i in range(DC)]
        wos = [E(_sbt(nc, f"awo{i}", [128, D], BF16)) for i in range(2)]
        brow = E(_sbt(nc, "abrow", [1, D], F32))
        bvrow = E(_sbt(nc, "abvrow", [1, D], F32))
        bqcol = E(_sbt(nc, "abqcol", [128, 8], F32))
        bkcol = E(_sbt(nc, "abkcol", [128, 8], F32))
        mstep = E(_sbt(nc, "amstep", [128, 4992], BF16)) if cfg["causal"] else None
        tok = None
        if cfg["causal"]:
            tok = load_small(c, "mask_step", mstep[:])
        if cfg.get("bout_row"):
            tok = load_small(c, cfg["bout_row"], brow[:])
        if cfg.get("bv_row"):
            tok = load_small(c, cfg["bv_row"], bvrow[:])
        if cfg.get("bq_col"):
            tok = load_small(c, cfg["bq_col"], bqcol[:])
            tok = load_small(c, cfg["bk_col"], bkcol[:])
        if tok is not None:
            p.dwait(tok)
        for i in range(NT):
            p.v(lambda e, i=i: e.memset(
                va[i][:].rearrange("p (h x) -> p h x", x=65)[:, :, 64:65], 1.0))
        p.barall()

        for g in range(nh // HG):
            for kind in ("q", "k"):
                dst = qa if kind == "q" else ka
                sc = cfg["scale"] if kind == "q" else 1.0
                bcol = None
                if cfg.get("b" + kind + "_col"):
                    bcol = bqcol if kind == "q" else bkcol
                tok = load_slab(c, cfg["w" + kind], wsl, 0, D,
                                cfg[kind + "off"] + gd * g, gd)
                p.dwait(tok)
                p.barall()
                with scope(c) as ps_es:
                    pps = [ps_es.enter_context(_pst(nc, f"aqp{i}", [128, 512], F32))
                           for i in range(2)]
                    done = {}
                    for j in range(2 * 4):
                        oc, nb = j // 4, j % 4
                        b = j % 2
                        sl = slice(512 * nb, 512 * (nb + 1))
                        ents = []
                        for i in range(DC):
                            ents.append((
                                lambda e, st, sp, i=i, oc=oc, sl=sl, b=b: e.matmul(
                                    pps[b][:], wsl[i][:, 128 * oc:128 * (oc + 1)],
                                    c.yT[i][:, sl], start=st, stop=sp),
                                [done.get(j - 2)] if i == 0 else None))
                        last = p.mmgroup(ents)
                        colj = (2 * g + oc)
                        if bcol is not None:
                            done[j] = p.v(lambda e, oc=oc, sl=sl, b=b, colj=colj, dst=dst,
                                          sc=sc, bcol=bcol: e.tensor_scalar(
                                dst[oc][:, sl], pps[b][:], bcol[:, colj:colj + 1], sc,
                                ALU.add, ALU.mult), wait=[last])
                        else:
                            done[j] = p.s(lambda e, oc=oc, sl=sl, b=b, dst=dst,
                                          sc=sc: e.activation(
                                dst[oc][:, sl], pps[b][:], AF.Copy, scale=sc), wait=[last])
                p.barall()
            tok = load_slab(c, cfg["wv"], wsl, 0, D, cfg["voff"] + gd * g, gd)
            p.dwait(tok)
            p.barall()
            with scope(c) as vses:
                vps = [vses.enter_context(_pst(nc, f"avp{i}", [128, 256], F32))
                       for i in range(2)]
                vdone = {}
                for nt in range(NT):
                    b = nt % 2
                    ents = []
                    for i in range(DC):
                        ents.append((
                            lambda e, st, sp, i=i, nt=nt, b=b: e.matmul(
                                vps[b][:], c.yT[i][:, 128 * nt:128 * (nt + 1)],
                                wsl[i][:, 0:256], start=st, stop=sp),
                            [vdone.get(nt - 2)] if i == 0 else None))
                    if cfg.get("bv_row"):
                        ents.append((
                            lambda e, st, sp, b=b, g=g: e.matmul(
                                vps[b][:], c.ones_row[0:1, 0:128],
                                bvrow[0:1, gd * g:gd * (g + 1)], start=st, stop=sp), None))
                    last = p.mmgroup(ents)
                    vdone[nt] = p.v(lambda e, nt=nt, b=b: e.tensor_copy(
                        va[nt][:].rearrange("p (h x) -> p h x", x=65)[:, :, 0:64],
                        vps[b][:].rearrange("p (h x) -> p h x", x=64)), wait=[last])
            p.barall()
            emit_negB(c, qa, ka, 64)
            with scope(c) as hses:
                sps = [hses.enter_context(_pst(nc, f"asp{i}", [128, 1024], F32))
                       for i in range(2)]
                ops = hses.enter_context(_pst(nc, "aop", [65, 1024], F32))
                rps = hses.enter_context(_pst(nc, "arp", [64, 1024], F32))
                expd = {}
                prev_ml = prev_rp = prev_cp = None
                for hh in range(HG):
                    ti, ro = (hh * 64) // 128, (hh * 64) % 128
                    for qb in range(2):
                        q0 = 1024 * qb
                        qsl = slice(q0, q0 + 1024)
                        kts = [kt for kt in range(NT)
                               if (not cfg["causal"]) or 128 * kt <= q0 + 1023]
                        avd = {}
                        for idx, kt in enumerate(kts):
                            b = idx % 2
                            mm = None
                            for hf in range(2):
                                mm = p.t(lambda e, kt=kt, b=b, ti=ti, ro=ro, q0=q0,
                                         hf=hf: e.matmul(
                                    sps[b][:, 512 * hf:512 * (hf + 1)],
                                    ka[ti][ro:ro + 64, 128 * kt:128 * (kt + 1)],
                                    qa[ti][ro:ro + 64,
                                           q0 + 512 * hf:q0 + 512 * (hf + 1)],
                                    start=True, stop=True),
                                    wait=([expd.get(idx - 2)] if hf == 0 else None))
                            ex = p.s(lambda e, b=b: e.activation(
                                pt[b][:], sps[b][:], AF.Exp, bias=c.negb[:, 0:1]),
                                wait=[mm, avd.get(idx - 2)])
                            expd[idx] = ex
                            wv = [ex]
                            if cfg["causal"] and 128 * kt + 127 > q0:
                                u0 = STEP_OFF + q0 - 128 * kt
                                dv = p.v(lambda e, b=b, u0=u0: e.tensor_mul(
                                    pt[b][:], pt[b][:], mstep[:, u0:u0 + 1024]),
                                    wait=[ex])
                                expd[idx] = dv
                                wv = [dv]
                            av = None
                            for hf in range(2):
                                av = p.t(lambda e, kt=kt, b=b, hh=hh, idx=idx, hf=hf,
                                         nn=len(kts): e.matmul(
                                    ops[:, 512 * hf:512 * (hf + 1)],
                                    va[kt][:, 65 * hh:65 * hh + 65],
                                    pt[b][:, 512 * hf:512 * (hf + 1)],
                                    start=(idx == 0), stop=(idx == nn - 1)),
                                    wait=(wv + [prev_ml] if (idx == 0 and hf == 0)
                                          else wv))
                            avd[idx] = av
                        expd = {-2: expd.get(len(kts) - 2), -1: expd.get(len(kts) - 1)}
                        rc = p.v(lambda e: e.reciprocal(c.row1[0:1, 0:1024],
                                                        ops[64:65, :]),
                                 wait=[avd[len(kts) - 1], prev_rp])
                        rp = None
                        for hf in range(2):
                            rp = p.t(lambda e, hf=hf: e.matmul(
                                rps[:, 512 * hf:512 * (hf + 1)],
                                c.ones_row[0:1, 0:64],
                                c.row1[0:1, 512 * hf:512 * (hf + 1)],
                                start=True, stop=True),
                                wait=([rc, prev_cp] if hf == 0 else None))
                        prev_rp = rp
                        cp = p.v(lambda e: e.tensor_copy(c.srec[0:64, 0:1024], rps[:]),
                                 wait=[rp, prev_ml])
                        prev_cp = cp
                        prev_ml = p.v(lambda e, ti=ti, ro=ro, qsl=qsl: e.tensor_mul(
                            ao[ti][ro:ro + 64, qsl], ops[0:64, :],
                            c.srec[0:64, 0:1024]), wait=[cp])
            tok = None
            for i in range(2):
                tok = p.dma(lambda e, i=i, g=g: e.dma_start(
                    out=wos[i][:], in_=c.hbm[cfg["wo"]][gd * g + 128 * i:
                                                        gd * g + 128 * (i + 1), :]),
                    q="gpsimd")
            p.dwait(tok)
            p.barall()
            with scope(c) as oses:
                pps = [oses.enter_context(_pst(nc, f"apo{i}", [128, 512], F32))
                       for i in range(2)]
                done = {}
                for j in range(DC * 4):
                    oc, nb = j // 4, j % 4
                    b = j % 2
                    sl = slice(512 * nb, 512 * (nb + 1))
                    ents = []
                    for i in range(2):
                        ents.append((
                            lambda e, st, sp, i=i, oc=oc, sl=sl, b=b: e.matmul(
                                pps[b][:], wos[i][:, 128 * oc:128 * (oc + 1)],
                                ao[i][:, sl], start=st, stop=sp),
                            [done.get(j - 2)] if i == 0 else None))
                    if g == 0 and cfg.get("bout_row"):
                        ents.append((lambda e, st, sp, oc=oc, b=b: e.matmul(
                            pps[b][:], brow[0:1, 128 * oc:128 * (oc + 1)],
                            c.ones_row[0:1, 0:512], start=st, stop=sp), None))
                    if g == 0 and cfg.get("add_y"):
                        ents.append((lambda e, st, sp, oc=oc, sl=sl, b=b: e.matmul(
                            pps[b][:], c.ident[:], c.yT[oc][:, sl], start=st, stop=sp),
                            None))
                    last = p.mmgroup(ents)
                    done[j] = p.v(lambda e, oc=oc, sl=sl, b=b: e.tensor_add(
                        c.xT[oc][:, sl], c.xT[oc][:, sl], pps[b][:]), wait=[last])
            p.barall()


def emit_moe(c):
    p, nc = c.p, c.nc
    with scope(c) as ses:
        E = ses.enter_context
        gatesT = E(_sbt(nc, "mgT", [8, N], BF16))
        grep = E(_sbt(nc, "mgrep", [128, N], F32))
        xe = [E(_sbt(nc, f"mxe{i}", [128, N], BF16)) for i in range(DC)]
        wsl = [E(_sbt(nc, f"mw{i}", [128, 512], BF16)) for i in range(DC)]
        w3sl = [E(_sbt(nc, f"mw3{i}", [128, 1024], BF16)) for i in range(DC)]
        rsl = [E(_sbt(nc, f"mr{i}", [128, 8], BF16)) for i in range(DC)]
        silu = E(_sbt(nc, "msil", [128, 512], F32))
        uscr = E(_sbt(nc, "musc", [128, 512], F32))
        erow = E(_sbt(nc, "mer", [1, 8], F32))
        ebrep = E(_sbt(nc, "meb", [128, 8], F32))
        sc1 = E(_sbt(nc, "msc1", [128, 8], F32))
        sc2 = E(_sbt(nc, "msc2", [128, 8], F32))
        top8 = E(_sbt(nc, "mtop", [128, 8], F32))
        gt = E(_sbt(nc, "mgt", [128, 8], BF16))
        sel8 = E(_sbt(nc, "msel", [8, 1024], BF16))
        tok = load_small(c, "sel8", sel8[:])
        tok = load_slab(c, "moe_routerT", rsl, 0, D, 0, 8)
        tok2 = load_small(c, "moe_ebias_row", erow[:])
        p.dwait(tok)
        p.dwait(tok2)
        p.barall()
        with scope(c) as rses:
            rps = rses.enter_context(_pst(nc, "mrp", [128, 8], F32))
            eps = rses.enter_context(_pst(nc, "mep", [128, 8], F32))
            gps = rses.enter_context(_pst(nc, "mgp", [8, 128], BF16))
            p.t(lambda e: e.matmul(eps[:], c.ones_row[0:1, 0:128], erow[0:1, :],
                                   start=True, stop=True))
            p.bar(("tensor", "vector"))
            p.v(lambda e: e.tensor_copy(ebrep[:], eps[:]))
            p.bar(("tensor", "vector"))
            prev_sg = prev_gm = prev_cp = prev_tr = None
            for nt in range(NT):
                ents = []
                for i in range(DC):
                    ents.append((lambda e, st, sp, i=i, nt=nt: e.matmul(
                        rps[:], c.yT[i][:, 128 * nt:128 * (nt + 1)], rsl[i][:, 0:8],
                        start=st, stop=sp),
                        ([prev_sg] if i == 0 else None)))
                last = p.mmgroup(ents)
                sg = p.s(lambda e: e.activation(sc1[:], rps[:], AF.Sigmoid),
                         wait=[last, prev_gm])
                prev_sg = sg
                t1 = p.v(lambda e: e.tensor_add(sc2[:], sc1[:], ebrep[:]), wait=[sg])
                t2 = p.v(lambda e: e.max(top8[:], sc2[:]), wait=[t1])
                t3 = p.v(lambda e: e.tensor_scalar(sc2[:], sc2[:], top8[:, 1:2], None,
                                                   ALU.is_ge), wait=[t2])
                gm = p.v(lambda e: e.tensor_mul(gt[:], sc2[:], sc1[:]),
                         wait=[t3, prev_tr])
                prev_gm = gm
                tr = p.t(lambda e: e.transpose(gps[:], gt[:], c.ident[:]),
                         wait=[gm, prev_cp])
                prev_tr = tr
                prev_cp = p.v(lambda e, nt=nt: e.tensor_copy(
                    gatesT[:, 128 * nt:128 * (nt + 1)], gps[:]), wait=[tr])
            p.bar()
        p.barall()
        for ex in range(12):
            gated = ex < 8
            if gated:
                with scope(c) as gses:
                    gpps = gses.enter_context(_pst(nc, "mgrp", [128, 512], F32))
                    for nb in range(4):
                        sl = slice(512 * nb, 512 * (nb + 1))
                        mmg = p.t(lambda e, sl=sl, ex=ex: e.matmul(
                            gpps[:], sel8[:, 128 * ex:128 * (ex + 1)], gatesT[:, sl],
                            start=True, stop=True))
                        p.v(lambda e, sl=sl: e.tensor_copy(grep[:, sl], gpps[:]),
                            wait=[mmg])
                        p.bar(("tensor", "vector"))
            tokw3 = None
            for i in range(DC):
                tokw3 = p.dma(lambda e, i=i, ex=ex: e.dma_start(
                    out=w3sl[i][:], in_=c.hbm["moe_w3T"][ex, 128 * i:128 * (i + 1), :]),
                    q="gpsimd", sem="Dgw")

            with scope(c) as ases:
                pa1 = [ases.enter_context(_pst(nc, f"mpa1{i}", [128, 512], F32))
                       for i in range(2)]
                pa2 = [ases.enter_context(_pst(nc, f"mpa2{i}", [128, 512], F32))
                       for i in range(2)]
                ldtok = {}
                a1rd = {}
                a2rd = {}
                xvd = {}
                slab_rd = {}
                for ch in range(9):
                    if ch < 8:
                        tok = None
                        for i in range(DC):
                            tok = p.dma(lambda e, i=i, ch=ch, ex=ex: e.dma_start(
                                out=wsl[i][:, 256 * (ch % 2):256 * (ch % 2) + 256],
                                in_=c.hbm["moe_w12T"][ex, 128 * i:128 * (i + 1),
                                                      256 * ch:256 * (ch + 1)]),
                                q="gpsimd", sem=("Dga" if ch % 2 == 0 else "Dgb"),
                                wait=([slab_rd.get(ch - 2)] if i == 0 else None))
                        ldtok[ch] = tok
                    if ch == 0:
                        continue
                    cc = ch - 1
                    co = 256 * (cc % 2)
                    for nb in range(4):
                        j = cc * 4 + nb
                        b = j % 2
                        sl = slice(512 * nb, 512 * (nb + 1))
                        ents1 = [(lambda e, st, sp, i=i, sl=sl, b=b, co=co: e.matmul(
                            pa1[b][:], wsl[i][:, co:co + 128], c.yT[i][:, sl],
                            start=st, stop=sp),
                            ([ldtok[cc], a1rd.get(j - 2)] if i == 0 else None))
                            for i in range(DC)]
                        l1 = p.mmgroup(ents1)
                        ents2 = [(lambda e, st, sp, i=i, sl=sl, b=b, co=co: e.matmul(
                            pa2[b][:], wsl[i][:, co + 128:co + 256], c.yT[i][:, sl],
                            start=st, stop=sp),
                            ([a2rd.get(j - 2)] if i == 0 else None))
                            for i in range(DC)]
                        l2 = p.mmgroup(ents2)
                        sv = p.s(lambda e, b=b: e.activation(silu[:], pa1[b][:],
                                                             AF.Silu),
                                 wait=[l1, a2rd.get(j - 1)])
                        a1rd[j] = sv
                        if gated:
                            uv = p.v(lambda e, b=b: e.tensor_mul(uscr[:], pa2[b][:],
                                                                 silu[:]),
                                     wait=[l2, sv, xvd.get(j - 1)])
                            xv = p.g(lambda e, cc=cc, sl=sl: e.tensor_mul(
                                xe[cc][:, sl], uscr[:], grep[:, sl]), wait=[uv])
                            a2rd[j] = uv
                            xvd[j] = xv
                        else:
                            xv = p.v(lambda e, cc=cc, sl=sl, b=b: e.scalar_tensor_tensor(
                                xe[cc][:, sl], pa2[b][:], 1.0, silu[:],
                                ALU.mult, ALU.mult), wait=[l2, sv])
                            a2rd[j] = xv
                            xvd[j] = xv
                        slab_rd[cc] = l2
            p.dwait(tokw3, eng="gpsimd")
            p.barall()
            with scope(c) as rses:
                pr = [rses.enter_context(_pst(nc, f"mpr{i}", [128, 512], F32))
                      for i in range(2)]
                done = {}
                for j in range(DC * 4):
                    dcb, nb = j // 4, j % 4
                    b = j % 2
                    sl = slice(512 * nb, 512 * (nb + 1))
                    ents = [(lambda e, st, sp, ch=ch2, dcb=dcb, sl=sl, b=b: e.matmul(
                        pr[b][:], w3sl[ch][:, 128 * dcb:128 * (dcb + 1)], xe[ch][:, sl],
                        start=st, stop=sp),
                        [done.get(j - 2)] if ch2 == 0 else None) for ch2 in range(DC)]
                    last = p.mmgroup(ents)
                    done[j] = p.v(lambda e, dcb=dcb, sl=sl, b=b: e.tensor_add(
                        c.xT[dcb][:, sl], c.xT[dcb][:, sl], pr[b][:]), wait=[last])
            p.barall()


def emit_outproj(c, wo_name, ao, rows0, nrows, with_bias_row=None, add_y=False):
    p, nc = c.p, c.nc
    nch = nrows // 128
    with scope(c) as ses:
        E = ses.enter_context
        wos = [E(_sbt(nc, f"opw{i}", [128, D], BF16)) for i in range(nch)]
        brow = E(_sbt(nc, "opbr", [1, D], F32))
        tok = None
        for i in range(nch):
            tok = p.dma(lambda e, i=i: e.dma_start(
                out=wos[i][:], in_=c.hbm[wo_name][rows0 + 128 * i:rows0 + 128 * (i + 1), :]),
                q="gpsimd")
        if with_bias_row:
            tok2 = load_small(c, with_bias_row, brow[:])
            p.dwait(tok2)
        p.dwait(tok)
        p.barall()
        pps = [E(_pst(nc, f"opp{i}", [128, 512], F32)) for i in range(2)]
        done = {}
        for j in range(DC * 4):
            oc, nb = j // 4, j % 4
            b = j % 2
            sl = slice(512 * nb, 512 * (nb + 1))
            ents = []
            for i in range(nch):
                ents.append((lambda e, st, sp, i=i, oc=oc, sl=sl, b=b: e.matmul(
                    pps[b][:], wos[i][:, 128 * oc:128 * (oc + 1)], ao[i][:, sl],
                    start=st, stop=sp), [done.get(j - 2)] if i == 0 else None))
            if with_bias_row:
                ents.append((lambda e, st, sp, oc=oc, b=b: e.matmul(
                    pps[b][:], brow[0:1, 128 * oc:128 * (oc + 1)],
                    c.ones_row[0:1, 0:512], start=st, stop=sp), None))
            if add_y:
                ents.append((lambda e, st, sp, oc=oc, sl=sl, b=b: e.matmul(
                    pps[b][:], c.ident[:], c.yT[oc][:, sl], start=st, stop=sp), None))
            last = p.mmgroup(ents)
            done[j] = p.v(lambda e, oc=oc, sl=sl, b=b: e.tensor_add(
                c.xT[oc][:, sl], c.xT[oc][:, sl], pps[b][:]), wait=[last])
    p.barall()


L0B_LIMIT = "full"


def emit_l0b(c):
    p, nc = c.p, c.nc
    SC = 1.0 / 8.0
    HG, gd = 4, 256
    with scope(c) as ses:
        E = ses.enter_context
        qr = [E(_sbt(nc, f"bq{i}", [128, N], BF16)) for i in range(2)]
        kr = [E(_sbt(nc, f"bk{i}", [128, N], BF16)) for i in range(2)]
        qs1 = E(_sbt(nc, "bs1t", [128, N], BF16))
        qs = [qs1, qs1]
        ao = [E(_sbt(nc, f"bo{i}", [128, N], BF16)) for i in range(2)]
        va = [E(_sbt(nc, f"bv{i}", [128, HG * 65], BF16)) for i in range(NT)]
        pt = [E(_sbt(nc, f"bp{i}", [128, 1024], BF16)) for i in range(2)]
        wsl = [E(_sbt(nc, f"bw{i}", [128, gd], BF16)) for i in range(DC)]
        bselT = [E(_sbt(nc, f"bbt{i}", [128, N], BF16)) for i in range(2)]
        cos2 = E(_sbt(nc, "bcos", [128, 2688], BF16))
        sin2 = E(_sbt(nc, "bsin", [128, 2560], BF16))
        pisb = E(_sbt(nc, "bpi", [128, 128], BF16))
        kl2 = E(_sbt(nc, "bkl", [128, N], BF16))
        eb2 = E(_sbt(nc, "beb", [128, N], BF16))
        mband = cos2
        mstep = sin2
        k1sb = E(_sbt(nc, "bk1", [128, 256], BF16))
        top8 = E(_sbt(nc, "bt8", [128, 8], F32))
        bsel = E(_sbt(nc, "bbs", [128, 256], BF16))
        load_small(c, "pi128T", pisb[:])
        tok = load_small(c, "ebig2", eb2[:])
        tokg = load_slab(c, "l0b_klod1gT2", [kl2], 0, 128, 0, 2048)
        p.dwait(tok)
        p.dwait(tokg)
        for i in range(NT):
            p.v(lambda e, i=i: e.memset(
                va[i][:].rearrange("p (h x) -> p h x", x=65)[:, :, 64:65], 1.0))
        p.barall()

        for g in range(4):
            load_cast(c, "rope_cos2", cos2[:, 0:N])
            tok = load_cast(c, "rope_sin2", sin2[:, 0:N])
            p.dwait(tok)
            p.barall()
            for kind in ("q", "k"):
                dst = qr if kind == "q" else kr
                sc = SC if kind == "q" else 1.0
                tok = load_slab(c, "l0b_w" + kind + "T", wsl, 0, D, gd * g, gd)
                p.dwait(tok)
                p.barall()
                with scope(c) as ps_es:
                    pps = [ps_es.enter_context(_pst(nc, f"bqp{i}", [128, 512], F32))
                           for i in range(2)]
                    pps2 = [ps_es.enter_context(_pst(nc, f"bqq{i}", [128, 512], F32))
                            for i in range(2)]
                    done = {}
                    pdone = {}
                    for j in range(2 * 4):
                        oc, nb = j // 4, j % 4
                        b = j % 2
                        sl = slice(512 * nb, 512 * (nb + 1))
                        ents = []
                        for i in range(DC):
                            ents.append((lambda e, st, sp, i=i, oc=oc, sl=sl, b=b:
                                         e.matmul(pps[b][:],
                                                  wsl[i][:, 128 * oc:128 * (oc + 1)],
                                                  c.yT[i][:, sl], start=st, stop=sp),
                                         [done.get(j - 2)] if i == 0 else None))
                        last = p.mmgroup(ents)
                        p.v(lambda e, oc=oc, sl=sl, b=b, sc=sc, dst=dst:
                            e.scalar_tensor_tensor(dst[oc][:, sl], pps[b][:], sc,
                                                   cos2[:, sl], ALU.mult, ALU.mult),
                            wait=[last])
                        e2 = p.v(lambda e, oc=oc, sl=sl, b=b, sc=sc:
                                 e.scalar_tensor_tensor(qs[oc][:, sl], pps[b][:], sc,
                                                        sin2[:, sl], ALU.mult, ALU.mult),
                                 wait=[pdone.get(j - 4)])
                        done[j] = e2
                        pm = p.t(lambda e, oc=oc, sl=sl, b=b: e.matmul(
                            pps2[b][:], pisb[:], qs[oc][:, sl], start=True, stop=True),
                            wait=[e2, pdone.get(j - 2)])
                        pdone[j] = p.v(lambda e, oc=oc, sl=sl, b=b, dst=dst: e.tensor_add(
                            dst[oc][:, sl], dst[oc][:, sl], pps2[b][:]), wait=[pm])
                p.barall()
            tok = load_slab(c, "l0b_wvT", wsl, 0, D, gd * g, gd)
            p.dwait(tok)
            p.barall()
            with scope(c) as vses:
                vps = [vses.enter_context(_pst(nc, f"bvp{i}", [128, 256], F32))
                       for i in range(2)]
                vdone = {}
                for nt in range(NT):
                    b = nt % 2
                    ents = []
                    for i in range(DC):
                        ents.append((lambda e, st, sp, i=i, nt=nt, b=b: e.matmul(
                            vps[b][:], c.yT[i][:, 128 * nt:128 * (nt + 1)],
                            wsl[i][:, 0:256], start=st, stop=sp),
                            [vdone.get(nt - 2)] if i == 0 else None))
                    last = p.mmgroup(ents)
                    vdone[nt] = p.v(lambda e, nt=nt, b=b: e.tensor_copy(
                        va[nt][:].rearrange("p (h x) -> p h x", x=65)[:, :, 0:64],
                        vps[b][:].rearrange("p (h x) -> p h x", x=64)), wait=[last])
            p.barall()
            if L0B_LIMIT == "qk":
                continue
            with scope(c) as kses:
                k1ps = kses.enter_context(_pst(nc, "bkp", [64, 64], F32))
                for h4 in range(4):
                    ti, ro = h4 // 2, 64 * (h4 % 2)
                    ents = []
                    for gg in range(32):
                        ents.append((lambda e, st, sp, gg=gg, ti=ti, ro=ro: e.matmul(
                            k1ps[:], kl2[ro:ro + 64, 64 * gg:64 * (gg + 1)],
                            kr[ti][ro:ro + 64, :].rearrange(
                                "p (t g) -> p g t", g=32)[:, gg, :],
                            start=st, stop=sp), None))
                    last = p.mmgroup(ents)
                    p.v(lambda e, h4=h4, ro=ro: e.tensor_copy(
                        k1sb[ro:ro + 64, 64 * h4:64 * (h4 + 1)], k1ps[:]), wait=[last])
                    p.bar(("tensor", "vector"))
            p.barall()
            if L0B_LIMIT == "pool":
                continue
            with scope(c) as sses:
                s1ps = [sses.enter_context(_pst(nc, f"bsp{i}", [128, 64], F32))
                        for i in range(4)]
                btps = [sses.enter_context(_pst(nc, f"btp{i}", [64, 128], BF16))
                        for i in range(2)]
                btcp = {}
                bsld = {}
                for nt in range(NT):
                    lasts = None
                    for h4 in range(4):
                        ti, ro = h4 // 2, 64 * (h4 % 2)
                        lasts = p.t(lambda e, h4=h4, ti=ti, ro=ro, nt=nt: e.matmul(
                            s1ps[h4][:],
                            qr[ti][ro:ro + 64, 128 * nt:128 * (nt + 1)],
                            k1sb[ro:ro + 64, 64 * h4:64 * (h4 + 1)],
                            start=True, stop=True), wait=[bsld.get(h4)])
                    bsl = None
                    for h4 in range(4):
                        mx = p.v(lambda e, h4=h4: e.max(top8[:], s1ps[h4][:]),
                                 wait=[lasts, bsl])
                        bsl = p.v(lambda e, h4=h4: e.tensor_scalar(
                            bsel[:, 64 * h4:64 * (h4 + 1)], s1ps[h4][:],
                            top8[:, 7:8], None, ALU.is_ge), wait=[mx])
                        bsld[h4] = bsl
                    for h4 in range(4):
                        ti, ro = h4 // 2, 64 * (h4 % 2)
                        b = h4 % 2
                        tr = p.t(lambda e, h4=h4, b=b: e.transpose(
                            btps[b][:], bsel[:, 64 * h4:64 * (h4 + 1)], c.ident[:]),
                            wait=[bsl, btcp.get(h4 - 2)])
                        btcp[h4] = p.v(lambda e, ti=ti, ro=ro, nt=nt, b=b: e.tensor_copy(
                            bselT[ti][ro:ro + 64, 128 * nt:128 * (nt + 1)], btps[b][:]),
                            wait=[tr])
                    btcp = {-2: btcp.get(2), -1: btcp.get(3)}
            p.barall()
            if L0B_LIMIT == "sel":
                continue
            emit_negB(c, qr, kr, 64)
            p.dma(lambda e: e.dma_start(out=mband[:],
                                        in_=c.hbm["mask_band"][:, 1024:3712]),
                  q="gpsimd")
            tok = p.dma(lambda e: e.dma_start(out=mstep[:],
                                              in_=c.hbm["mask_step"][:, 1408:3968]),
                        q="gpsimd")
            p.dwait(tok)
            p.barall()
            with scope(c) as hses:
                sps = [hses.enter_context(_pst(nc, f"bss{i}", [128, 1024], F32))
                       for i in range(2)]
                ops = hses.enter_context(_pst(nc, "bop", [65, 1024], F32))
                rps = hses.enter_context(_pst(nc, "brp", [64, 1024], F32))
                expd = {}
                prev_ml = prev_rp = prev_cp = None
                for h4 in range(4):
                    ti, ro = h4 // 2, 64 * (h4 % 2)
                    for qb in range(2):
                        q0 = 1024 * qb
                        qsl = slice(q0, q0 + 1024)
                        tiles = []
                        for kt in range(NT):
                            k0 = 128 * kt
                            if k0 <= q0 + 1023 and k0 + 127 >= q0 - 638:
                                tiles.append((kt, "band", False))
                        for kt in range(NT):
                            k0 = 128 * kt
                            if k0 <= q0 + 511:
                                tiles.append((kt, "below", k0 <= q0 - 639))
                        avd = {}
                        for idx, tspec in enumerate(tiles):
                            kt, mode, pure = tspec
                            k0 = 128 * kt
                            b = idx % 2
                            mm = None
                            for hf in range(2):
                                if mode == "band":
                                    mm = p.t(lambda e, kt=kt, b=b, ti=ti, ro=ro, q0=q0,
                                             hf=hf: e.matmul(
                                        sps[b][:, 512 * hf:512 * (hf + 1)],
                                        kr[ti][ro:ro + 64, 128 * kt:128 * (kt + 1)],
                                        qr[ti][ro:ro + 64,
                                               q0 + 512 * hf:q0 + 512 * (hf + 1)],
                                        start=True, stop=True),
                                        wait=([expd.get(idx - 2)] if hf == 0 else None))
                                else:
                                    p.t(lambda e, kt=kt, b=b, ti=ti, ro=ro, q0=q0,
                                        hf=hf: e.matmul(
                                        sps[b][:, 512 * hf:512 * (hf + 1)],
                                        kr[ti][ro:ro + 64, 128 * kt:128 * (kt + 1)],
                                        qr[ti][ro:ro + 64,
                                               q0 + 512 * hf:q0 + 512 * (hf + 1)],
                                        start=True, stop=False),
                                        wait=([expd.get(idx - 2)] if hf == 0 else None))
                                    mm = p.t(lambda e, kt=kt, b=b, ti=ti, ro=ro, q0=q0,
                                             hf=hf: e.matmul(
                                        sps[b][:, 512 * hf:512 * (hf + 1)],
                                        eb2[ro:ro + 64, 128 * kt:128 * (kt + 1)],
                                        bselT[ti][ro:ro + 64,
                                                  q0 + 512 * hf:q0 + 512 * (hf + 1)],
                                        start=False, stop=True))
                            bias = c.negb[:, 0:1] if mode == "band" else c.negb[:, 1:2]
                            ex = p.s(lambda e, b=b, bias=bias: e.activation(
                                pt[b][:], sps[b][:], AF.Exp, bias=bias),
                                wait=[mm, avd.get(idx - 2)])
                            expd[idx] = ex
                            wv = [ex]
                            if mode == "band":
                                u0 = 1024 + q0 - k0
                                dv = p.v(lambda e, b=b, u0=u0: e.tensor_mul(
                                    pt[b][:], pt[b][:], mband[:, u0:u0 + 1024]),
                                    wait=[ex])
                                expd[idx] = dv
                                wv = [dv]
                            elif not pure:
                                u0 = STEP_OFF + q0 - k0 - 512 - 1408
                                dv = p.v(lambda e, b=b, u0=u0: e.tensor_mul(
                                    pt[b][:], pt[b][:], mstep[:, u0:u0 + 1024]),
                                    wait=[ex])
                                expd[idx] = dv
                                wv = [dv]
                            av = None
                            for hf in range(2):
                                av = p.t(lambda e, kt=kt, b=b, h4=h4, idx=idx, hf=hf,
                                         nn=len(tiles): e.matmul(
                                    ops[:, 512 * hf:512 * (hf + 1)],
                                    va[kt][:, 65 * h4:65 * h4 + 65],
                                    pt[b][:, 512 * hf:512 * (hf + 1)],
                                    start=(idx == 0), stop=(idx == nn - 1)),
                                    wait=(wv + [prev_ml] if (idx == 0 and hf == 0)
                                          else wv))
                            avd[idx] = av
                        expd = {-2: expd.get(len(tiles) - 2),
                                -1: expd.get(len(tiles) - 1)}
                        rc = p.v(lambda e: e.reciprocal(c.row1[0:1, 0:1024],
                                                        ops[64:65, :]),
                                 wait=[avd[len(tiles) - 1], prev_rp])
                        rp = None
                        for hf in range(2):
                            rp = p.t(lambda e, hf=hf: e.matmul(
                                rps[:, 512 * hf:512 * (hf + 1)],
                                c.ones_row[0:1, 0:64],
                                c.row1[0:1, 512 * hf:512 * (hf + 1)],
                                start=True, stop=True),
                                wait=([rc, prev_cp] if hf == 0 else None))
                        prev_rp = rp
                        cp = p.v(lambda e: e.tensor_copy(c.srec[0:64, 0:1024], rps[:]),
                                 wait=[rp, prev_ml])
                        prev_cp = cp
                        prev_ml = p.v(lambda e, ti=ti, ro=ro, qsl=qsl: e.tensor_mul(
                            ao[ti][ro:ro + 64, qsl], ops[0:64, :],
                            c.srec[0:64, 0:1024]), wait=[cp])
            p.barall()
            if L0B_LIMIT == "heads":
                continue
            emit_outproj(c, "l0b_woT", ao, gd * g, gd)


def load_cast(c, hbm_name, dst_ap, idx0=None):
    src = c.hbm[hbm_name]
    if idx0 is not None:
        src = src[idx0]
    return c.p.dma(lambda e, src=src, dst_ap=dst_ap: e.dma_start(out=dst_ap, in_=src),
                   q="gpsimd")


def emit_l1a(c):
    p, nc = c.p, c.nc
    SC = float(1.0 / np.sqrt(160.0))
    HG = 2
    with scope(c) as ses:
        E = ses.enter_context
        cq = E(_sbt(nc, "lcq", [33, N], BF16))
        ckv = E(_sbt(nc, "lckv", [33, N], BF16))
        krr = E(_sbt(nc, "lkr", [32, N], BF16))
        wdq = [E(_sbt(nc, f"lwdq{i}", [128, 96], BF16)) for i in range(DC)]
        wuq = E(_sbt(nc, "lwuq", [32, 1024], BF16))
        wuk = E(_sbt(nc, "lwuk", [32, 1024], BF16))
        wuv = E(_sbt(nc, "lwuv", [33, 1024], BF16))
        wqr = E(_sbt(nc, "lwqr", [32, 256], BF16))
        bdq = E(_sbt(nc, "lbdq", [32, 2], F32))
        bkr = E(_sbt(nc, "lbkr", [32, 1], F32))
        buq = E(_sbt(nc, "lbuq", [128, 8], F32))
        buk = E(_sbt(nc, "lbuk", [128, 8], F32))
        bqr = E(_sbt(nc, "lbqr", [64, 4], F32))
        load_slab(c, "l1a_wdq", wdq, 0, D, 0, 32)
        load_slab(c, "l1a_wdkv", wdq, 0, D, 0, 32, dcol0=32)
        load_slab(c, "l1a_wkr", wdq, 0, D, 0, 32, dcol0=64)
        load_cast(c, "l1a_wuq", wuq[0:32, :])
        load_cast(c, "l1a_wuk", wuk[0:32, :])
        load_cast(c, "l1a_wuv_aug", wuv[0:33, :])
        tok = load_cast(c, "l1a_wqr", wqr[0:32, :])
        load_small(c, "l1a_bdq_col", bdq[0:32, 0:1])
        load_small(c, "l1a_bdkv_col", bdq[0:32, 1:2])
        load_small(c, "l1a_bkr_col", bkr[0:32, 0:1])
        load_small(c, "l1a_buq_col", buq[:])
        load_small(c, "l1a_buk_col", buk[:])
        tok2 = load_small(c, "l1a_bqr_col", bqr[0:64, :])
        p.dwait(tok)
        p.dwait(tok2)
        p.v(lambda e: e.memset(cq[32:33, :], 1.0))
        p.v(lambda e: e.memset(ckv[32:33, :], 1.0))
        p.barall()
        with scope(c) as cses:
            cps = [cses.enter_context(_pst(nc, f"lcp{i}", [32, 512], F32))
                   for i in range(2)]
            done = {}
            specs = [(0, cq, bdq[0:32, 0:1]), (32, ckv, bdq[0:32, 1:2]),
                     (64, krr, bkr[0:32, 0:1])]
            for jj, (col0, dst, bias) in enumerate(specs):
                for nb in range(4):
                    j = jj * 4 + nb
                    b = j % 2
                    sl = slice(512 * nb, 512 * (nb + 1))
                    ents = []
                    for i in range(DC):
                        ents.append((lambda e, st, sp, i=i, col0=col0, sl=sl, b=b:
                                     e.matmul(cps[b][:], wdq[i][:, col0:col0 + 32],
                                              c.yT[i][:, sl], start=st, stop=sp),
                                     [done.get(j - 2)] if i == 0 else None))
                    last = p.mmgroup(ents)
                    done[j] = p.v(lambda e, sl=sl, b=b, dst=dst, bias=bias:
                                  e.tensor_scalar(dst[0:32, sl], cps[b][:], bias, None,
                                                  ALU.add), wait=[last])
        p.barall()
        with scope(c) as gses:
            E2 = gses.enter_context
            qc = [E2(_sbt(nc, f"lqc{i}", [128, N], BF16)) for i in range(HG)]
            kc = [E2(_sbt(nc, f"lkc{i}", [128, N], BF16)) for i in range(HG)]
            ao = [E2(_sbt(nc, f"lao{i}", [128, N], BF16)) for i in range(HG)]
            qrt = [E2(_sbt(nc, f"lqr{i}", [32, N], BF16)) for i in range(2)]
            va = [E2(_sbt(nc, f"lva{i}", [128, 256], BF16)) for i in range(NT)]
            pt = [E2(_sbt(nc, f"lpt{i}", [128, 1024], BF16)) for i in range(2)]
            for g in range(4):
                with scope(c) as ps_es:
                    pps = [ps_es.enter_context(_pst(nc, f"lqp{i}", [128, 512], F32))
                           for i in range(2)]
                    done = {}
                    for jj, (wsb, dst, bias, sc) in enumerate(
                            ((wuq, qc, buq, SC), (wuk, kc, buk, 1.0))):
                        for j2 in range(HG * 4):
                            oc, nb = j2 // 4, j2 % 4
                            j = jj * 8 + j2
                            b = j % 2
                            sl = slice(512 * nb, 512 * (nb + 1))
                            col = 256 * g + 128 * oc
                            src = cq if jj == 0 else ckv
                            last = p.t(lambda e, b=b, sl=sl, col=col, wsb=wsb, src=src:
                                       e.matmul(pps[b][:], wsb[0:32, col:col + 128],
                                                src[0:32, sl], start=True, stop=True),
                                       wait=[done.get(j - 2)])
                            colj = (256 * g) // 128 + oc
                            done[j] = p.v(lambda e, oc=oc, sl=sl, b=b, colj=colj,
                                          dst=dst, bias=bias, sc=sc: e.tensor_scalar(
                                dst[oc][:, sl], pps[b][:], bias[:, colj:colj + 1], sc,
                                ALU.add, ALU.mult), wait=[last])
                    for nb in range(4):
                        b = nb % 2
                        sl = slice(512 * nb, 512 * (nb + 1))
                        last = p.t(lambda e, sl=sl, g=g, b=b: e.matmul(
                            pps[b][0:64, :], wqr[0:32, 64 * g:64 * (g + 1)],
                            cq[0:32, sl], start=True, stop=True),
                            wait=[done.get(14 + nb)])
                        for h2 in range(2):
                            done[16 + nb] = p.v(
                                lambda e, sl=sl, g=g, b=b, h2=h2: e.tensor_scalar(
                                    qrt[h2][0:32, sl], pps[b][32 * h2:32 * h2 + 32, :],
                                    bqr[32 * h2:32 * h2 + 32, g:g + 1], SC,
                                    ALU.add, ALU.mult), wait=[last])
                    vdone = {}
                    for nt in range(NT):
                        b = nt % 2
                        last = p.t(lambda e, nt=nt, g=g, b=b: e.matmul(
                            pps[b][:, 0:256], ckv[0:33, 128 * nt:128 * (nt + 1)],
                            wuv[0:33, 256 * g:256 * (g + 1)], start=True, stop=True),
                            wait=[done.get(18 + nt), vdone.get(nt - 2)])
                        vdone[nt] = p.v(lambda e, nt=nt, b=b: e.tensor_copy(
                            va[nt][:], pps[b][:, 0:256]), wait=[last])
                        done[18 + nt] = vdone[nt]
                p.barall()
                emit_negB(c, qc + qrt, kc + [krr], 160)
                with scope(c) as hses:
                    sps = [hses.enter_context(_pst(nc, f"lsp{i}", [128, 1024], F32))
                           for i in range(2)]
                    ops = hses.enter_context(_pst(nc, "lop", [128, 1024], F32))
                    sm1 = hses.enter_context(_pst(nc, "lsm", [1, 1024], F32))
                    expd = {}
                    prev_ml = None
                    for h2 in range(HG):
                        ro = 32 * h2
                        for qb in range(2):
                            q0 = 1024 * qb
                            qsl = slice(q0, q0 + 1024)
                            avd = {}
                            for kt in range(NT):
                                b = kt % 2
                                mm = None
                                for hf in range(2):
                                    p.t(lambda e, kt=kt, b=b, h2=h2, q0=q0, hf=hf:
                                        e.matmul(
                                        sps[b][:, 512 * hf:512 * (hf + 1)],
                                        kc[h2][:, 128 * kt:128 * (kt + 1)],
                                        qc[h2][:, q0 + 512 * hf:q0 + 512 * (hf + 1)],
                                        start=True, stop=False),
                                        wait=([expd.get(kt - 2)] if hf == 0 else None))
                                    mm = p.t(lambda e, kt=kt, b=b, h2=h2, q0=q0, hf=hf:
                                             e.matmul(
                                        sps[b][:, 512 * hf:512 * (hf + 1)],
                                        krr[0:32, 128 * kt:128 * (kt + 1)],
                                        qrt[h2][0:32,
                                                q0 + 512 * hf:q0 + 512 * (hf + 1)],
                                        start=False, stop=True))
                                ex = p.s(lambda e, b=b: e.activation(
                                    pt[b][:], sps[b][:], AF.Exp, bias=c.negb[:, 0:1]),
                                    wait=[mm, avd.get(kt - 2)])
                                expd[kt] = ex
                                av = None
                                for hf in range(2):
                                    p.t(lambda e, kt=kt, b=b, h2=h2, hf=hf: e.matmul(
                                        ops[:, 512 * hf:512 * (hf + 1)],
                                        va[kt][:, 128 * h2:128 * (h2 + 1)],
                                        pt[b][:, 512 * hf:512 * (hf + 1)],
                                        start=(kt == 0), stop=(kt == NT - 1)),
                                        wait=([ex, prev_ml] if (kt == 0 and hf == 0)
                                              else [ex]))
                                    av = p.t(lambda e, kt=kt, b=b, hf=hf: e.matmul(
                                        sm1[:, 512 * hf:512 * (hf + 1)],
                                        c.ones_col_bf[:],
                                        pt[b][:, 512 * hf:512 * (hf + 1)],
                                        start=(kt == 0), stop=(kt == NT - 1)))
                                avd[kt] = av
                            rc = p.v(lambda e: e.reciprocal(c.row1[0:1, 0:1024],
                                                            sm1[:]),
                                     wait=[avd[NT - 1]])
                            rp = None
                            for hf in range(2):
                                rp = p.t(lambda e, hf=hf: e.matmul(
                                    sps[0][0:128, 512 * hf:512 * (hf + 1)],
                                    c.ones_row[0:1, 0:128],
                                    c.row1[0:1, 512 * hf:512 * (hf + 1)],
                                    start=True, stop=True),
                                    wait=([rc, expd.get(NT - 2)] if hf == 0 else None))
                            cp = p.v(lambda e: e.tensor_copy(c.srec[:, 0:1024],
                                                             sps[0][:]),
                                     wait=[rp, prev_ml])
                            ml = p.v(lambda e, h2=h2, qsl=qsl: e.tensor_mul(
                                ao[h2][:, qsl], ops[:], c.srec[:, 0:1024]),
                                wait=[cp])
                            prev_ml = ml
                            expd = {-2: ml, -1: expd.get(NT - 1)}
                p.barall()
                emit_outproj(c, "l1a_wo", ao, 256 * g, 256,
                             with_bias_row=("l1a_bo_row" if g == 0 else None))


def emit_smoe(c):
    p, nc = c.p, c.nc
    with scope(c) as ses:
        E = ses.enter_context
        gcol = E(_sbt(nc, "sgc", [128, 8], F32))
        bcol = E(_sbt(nc, "sbc", [128, 8], F32))
        sscal = E(_sbt(nc, "ssc", [1, 1], F32))
        tok = load_small(c, "sm_gamma_col", gcol[:])
        tok = load_small(c, "sm_beta_col", bcol[:])
        tok = load_small(c, "sm_scale", sscal[:])
        p.dwait(tok)
        p.barall()
        emit_ln2(c, c.yT, c.yT, gcol=gcol, bcol=bcol)
        musb = [E(_sbt(nc, f"smu{i}", [128, 256], BF16)) for i in range(DC)]
        rmu = E(_sbt(nc, "srmu", [128, 2], F32))
        logT = [E(_sbt(nc, f"slg{i}", [128, N], BF16)) for i in range(2)]
        combT = [E(_sbt(nc, f"scb{i}", [128, N], BF16)) for i in range(2)]
        dtok = [E(_sbt(nc, f"sdt{i}", [128, 256], BF16)) for i in range(NT)]
        siT = [E(_sbt(nc, f"ssi{i}", [128, 256], BF16)) for i in range(DC)]
        hT = E(_sbt(nc, "shT", [128, 512], BF16))
        SO = [E(_sbt(nc, f"sso{i}", [128, 1024], BF16)) for i in range(2)]
        mcol = E(_sbt(nc, "smc", [128, 1], F32))
        scol = E(_sbt(nc, "ssm", [128, 1], F32))
        srow1 = E(_sbt(nc, "srow1", [1, 2048], F32))
        srow2 = E(_sbt(nc, "srow2", [1, 2048], F32))
        tok = load_slab(c, "sm_mu", musb, 0, D, 0, 256)
        p.dwait(tok)
        p.barall()
        pre_es = ses.enter_context(ExitStack())
        rnrep = pre_es.enter_context(_sbt(nc, "srn", [128, N], F32))
        dispP = [pre_es.enter_context(_sbt(nc, f"sdp{i}", [128, N], BF16))
                 for i in range(2)]
        sqscr = [pre_es.enter_context(_sbt(nc, f"ssq{i}", [128, N], BF16))
                 for i in range(2)]
        with scope(c) as nses:
            psn = nses.enter_context(_pst(nc, "spn", [1, N], F32))
            mm_done = {}
            for i in range(DC):
                b = i % 2
                sq = p.s(lambda e, i=i, b=b: e.activation(sqscr[b][:], c.yT[i][:],
                                                          AF.Square),
                         wait=[mm_done.get(i - 2)])
                last = None
                for nb in range(4):
                    sl = slice(512 * nb, 512 * (nb + 1))
                    last = p.t(lambda e, b=b, sl=sl, i=i: e.matmul(
                        psn[:, sl], c.ones_col_bf[:], sqscr[b][:, sl],
                        start=(i == 0), stop=(i == DC - 1)), wait=[sq])
                mm_done[i] = last
            p.bar()
            p.s(lambda e: e.activation(srow1[:], psn[:], AF.Ln))
            p.bar(("scalar",))
            p.s(lambda e: e.activation(srow1[:], srow1[:], AF.Exp, scale=0.5))
            p.bar(("vector", "scalar"))
            p.v(lambda e: e.tensor_scalar_max(srow1[:], srow1[:], 1e-12))
            p.bar(("vector",))
            p.v(lambda e: e.reciprocal(srow2[:], srow1[:]))
            p.bar(("vector",))
            p.v(lambda e: e.tensor_scalar_mul(srow2[:], srow2[:], sscal[0:1, 0:1]))
            p.bar()
        p.barall()
        with scope(c) as nses:
            psr = nses.enter_context(_pst(nc, "spr", [128, N], F32))
            for nb in range(4):
                sl = slice(512 * nb, 512 * (nb + 1))
                p.t(lambda e, sl=sl: e.matmul(psr[:, sl], c.ones_row[0:1, 0:128],
                                              srow2[:, sl], start=True, stop=True))
            p.bar(("tensor", "vector"))
            p.v(lambda e: e.tensor_copy(rnrep[:], psr[:]))
            p.bar(("tensor", "vector"))
        p.barall()
        with scope(c) as mses:
            psm = mses.enter_context(_pst(nc, "spm", [1, 256], F32))
            ptm = mses.enter_context(_pst(nc, "ptm", [128, 4], BF16))
            mm_done = {}
            for i in range(DC):
                b = i % 2
                sq = p.s(lambda e, i=i, b=b: e.activation(sqscr[b][:, 0:256],
                                                          musb[i][:], AF.Square),
                         wait=[mm_done.get(i - 2)])
                mm_done[i] = p.t(lambda e, b=b, i=i: e.matmul(
                    psm[:], c.ones_col_bf[:], sqscr[b][:, 0:256],
                    start=(i == 0), stop=(i == DC - 1)), wait=[sq])
            p.bar()
            p.s(lambda e: e.activation(srow1[:, 0:256], psm[:], AF.Ln))
            p.bar(("scalar",))
            p.s(lambda e: e.activation(srow1[:, 0:256], srow1[:, 0:256], AF.Exp,
                                       scale=0.5))
            p.bar(("vector", "scalar"))
            p.v(lambda e: e.tensor_scalar_max(srow1[:, 0:256], srow1[:, 0:256], 1e-12))
            p.bar(("vector",))
            p.v(lambda e: e.reciprocal(srow2[:, 0:256], srow1[:, 0:256]))
            p.bar(("vector", "scalar"))
            p.v(lambda e: e.tensor_copy(sqscr[0][0:1, 0:256], srow2[:, 0:256]))
            p.bar(("tensor", "vector"))
            for a in range(2):
                p.t(lambda e, a=a: e.transpose(
                    ptm[:, 2 * a:2 * a + 1], sqscr[0][0:1, 128 * a:128 * (a + 1)],
                    c.ident[0:1, 0:1]))
            p.bar(("tensor", "vector"))
            for a in range(2):
                p.v(lambda e, a=a: e.tensor_copy(rmu[:, a:a + 1], ptm[:, 2 * a:2 * a + 1]))
            p.bar(("tensor", "vector"))
        p.barall()
        with scope(c) as lses:
            psl = [lses.enter_context(_pst(nc, f"spl{i}", [128, 512], F32))
                   for i in range(2)]
            done = {}
            for j in range(2 * 4):
                a, nb = j // 4, j % 4
                b = j % 2
                sl = slice(512 * nb, 512 * (nb + 1))
                ents = []
                for i in range(DC):
                    ents.append((lambda e, st, sp, i=i, a=a, sl=sl, b=b: e.matmul(
                        psl[b][:], musb[i][:, 128 * a:128 * (a + 1)], c.yT[i][:, sl],
                        start=st, stop=sp), [done.get(j - 2)] if i == 0 else None))
                last = p.mmgroup(ents)
                done[j] = p.v(lambda e, a=a, sl=sl, b=b: e.scalar_tensor_tensor(
                    logT[a][:, sl], psl[b][:], rmu[:, a:a + 1], rnrep[:, sl],
                    ALU.mult, ALU.mult), wait=[last])
        p.barall()
        dtk = None
        for a in range(2):
            t1 = p.v(lambda e, a=a: e.tensor_reduce(mcol[:], logT[a][:], AX.X, ALU.max),
                     wait=[dtk])
            t2 = p.v(lambda e: e.tensor_scalar_mul(mcol[:], mcol[:], -1.0), wait=[t1])
            ex = p.s(lambda e, a=a: e.activation(dispP[a][:], logT[a][:], AF.Exp,
                                                 bias=mcol[:, 0:1], accum_out=scol[:]),
                     wait=[t2, dtk])
            t3 = p.v(lambda e: e.reciprocal(scol[:], scol[:]), wait=[ex])
            dtk = p.v(lambda e, a=a: e.tensor_scalar_mul(dispP[a][:], dispP[a][:],
                                                         scol[:, 0:1]), wait=[t3])
        p.barall()
        with scope(c) as tses:
            ptt = [tses.enter_context(_pst(nc, f"ptt{i}", [128, 128], BF16))
                   for i in range(2)]
            ctok = tses.enter_context(_sbt(nc, "sct", [128, 256], F32))
            ctokb = tses.enter_context(_sbt(nc, "sctb", [128, 256], BF16))
            cpd = {}
            jj = 0
            prev_exp = prev_tr3 = None
            for nt in range(NT):
                for a in range(2):
                    b = jj % 2
                    tr = p.t(lambda e, a=a, nt=nt, b=b: e.transpose(
                        ptt[b][:], dispP[a][:, 128 * nt:128 * (nt + 1)], c.ident[:]),
                        wait=[cpd.get(jj - 2)])
                    cpd[jj] = p.v(lambda e, a=a, nt=nt, b=b: e.tensor_copy(
                        dtok[nt][:, 128 * a:128 * (a + 1)], ptt[b][:]), wait=[tr])
                    jj += 1
                cts = []
                for a in range(2):
                    b = jj % 2
                    tr = p.t(lambda e, a=a, nt=nt, b=b: e.transpose(
                        ptt[b][:], logT[a][:, 128 * nt:128 * (nt + 1)], c.ident[:]),
                        wait=[cpd.get(jj - 2), prev_exp])
                    cpd[jj] = p.v(lambda e, a=a, b=b: e.tensor_copy(
                        ctok[:, 128 * a:128 * (a + 1)], ptt[b][:]), wait=[tr])
                    cts.append(cpd[jj])
                    jj += 1
                t1 = p.v(lambda e: e.tensor_reduce(mcol[:], ctok[:], AX.X, ALU.max),
                         wait=[prev_exp, cts[-1]])
                ng = p.v(lambda e: e.tensor_scalar_mul(mcol[:], mcol[:], -1.0),
                         wait=[t1])
                ex = p.s(lambda e: e.activation(ctokb[:], ctok[:], AF.Exp,
                                                bias=mcol[:, 0:1], accum_out=scol[:]),
                         wait=[ng, cts[-1], prev_tr3])
                prev_exp = ex
                t2 = p.v(lambda e: e.reciprocal(scol[:], scol[:]), wait=[ex])
                ml = p.v(lambda e: e.tensor_scalar_mul(ctokb[:], ctokb[:],
                                                       scol[:, 0:1]), wait=[t2])
                for a in range(2):
                    b = jj % 2
                    tr = p.t(lambda e, a=a, b=b: e.transpose(
                        ptt[b][:], ctokb[:, 128 * a:128 * (a + 1)], c.ident[:]),
                        wait=[ml, cpd.get(jj - 2)])
                    prev_tr3 = tr
                    cpd[jj] = p.v(lambda e, a=a, nt=nt, b=b: e.tensor_copy(
                        combT[a][:, 128 * nt:128 * (nt + 1)], ptt[b][:]), wait=[tr])
                    jj += 1
        p.barall()
        p.seal()
        pre_es.close()
        with scope(c) as sses:
            psi = [sses.enter_context(_pst(nc, f"psi{i}", [128, 256], F32))
                   for i in range(2)]
            ptx = [sses.enter_context(_pst(nc, f"ptx{i}", [128, 128], BF16))
                   for i in range(2)]
            xnt = [sses.enter_context(_sbt(nc, f"sxn{i}", [128, 1024], BF16))
                   for i in range(NT)]
            cpd = {}
            for j in range(NT * DC):
                nt, dc = j // DC, j % DC
                b = j % 2
                tr = p.t(lambda e, dc=dc, nt=nt, b=b: e.transpose(
                    ptx[b][:], c.yT[dc][:, 128 * nt:128 * (nt + 1)], c.ident[:]),
                    wait=[cpd.get(j - 2)])
                cpd[j] = p.v(lambda e, dc=dc, nt=nt, b=b: e.tensor_copy(
                    xnt[nt][:, 128 * dc:128 * (dc + 1)], ptx[b][:]), wait=[tr])
            p.barall()
            done = {}
            for dc in range(DC):
                b = dc % 2
                ents = []
                for nt in range(NT):
                    ents.append((lambda e, st, sp, dc=dc, nt=nt, b=b: e.matmul(
                        psi[b][:], xnt[nt][:, 128 * dc:128 * (dc + 1)],
                        dtok[nt][:, 0:256], start=st, stop=sp),
                        [done.get(dc - 2)] if nt == 0 else None))
                last = p.mmgroup(ents)
                done[dc] = p.v(lambda e, dc=dc, b=b: e.tensor_copy(
                    siT[dc][:], psi[b][:]), wait=[last])
        p.barall()
        with scope(c) as eses:
            E3 = eses.enter_context
            w1sl = [E3(_sbt(nc, f"sw1{i}", [128, 256], BF16)) for i in range(DC)]
            w2sl = [E3(_sbt(nc, f"sw2{i}", [128, 1024], BF16)) for i in range(2)]
            b1row = E3(_sbt(nc, "sb1", [1, 256], F32))
            b2row = E3(_sbt(nc, "sb2r", [1, 1024], F32))
            hgbf = E3(_sbt(nc, "shg", [16, 256], BF16))
            psh = E3(_pst(nc, "psh", [16, 256], F32))
            pth = E3(_pst(nc, "pth", [128, 16], BF16))
            pso = [E3(_pst(nc, f"pso{i}", [128, 512], F32)) for i in range(4)]
            ptxx = E3(_pst(nc, "ptxx", [128, 128], BF16))
            soT = [E3(_sbt(nc, f"ssoT{i}", [128, 256], BF16)) for i in range(DC)]
            for ex in range(16):
                p.barall()
                load_slab(c, "sm_w1", w1sl, 0, D, 0, 256, idx0=ex)
                load_slab(c, "sm_w2", w2sl, 0, 256, 0, 1024, idx0=ex)
                tok = c.p.dma(lambda e, ex=ex: e.dma_start(out=b1row[:],
                                                           in_=c.hbm["sm_b1"][ex]))
                tok2 = c.p.dma(lambda e, ex=ex: e.dma_start(out=b2row[:],
                                                            in_=c.hbm["sm_b2"][ex]))
                p.dwait(tok2)
                p.barall()
                ents = []
                for dc in range(DC):
                    ents.append((lambda e, st, sp, dc=dc, ex=ex: e.matmul(
                        psh[:], siT[dc][:, 16 * ex:16 * (ex + 1)], w1sl[dc][:, 0:256],
                        start=st, stop=sp), None))
                ents.append((lambda e, st, sp: e.matmul(
                    psh[:], c.ones_row[0:1, 0:16], b1row[0:1, :],
                    start=st, stop=sp), None))
                last = p.mmgroup(ents)
                gl = p.s(lambda e: e.activation(hgbf[:], psh[:], AF.Gelu), wait=[last])
                p.bar(("tensor", "vector", "scalar"))
                for a in range(2):
                    tr = p.t(lambda e, a=a: e.transpose(
                        pth[:], hgbf[0:16, 128 * a:128 * (a + 1)], c.ident[0:16, 0:16]))
                    p.v(lambda e, a=a: e.tensor_copy(
                        hT[:, 16 * a:16 * a + 16], pth[:, 0:16]), wait=[tr])
                    p.bar(("tensor", "vector"))
                for dc in range(DC):
                    ents = []
                    for a in range(2):
                        ents.append((lambda e, st, sp, a=a, dc=dc, ex=ex: e.matmul(
                            pso[dc // 2][:, 256 * (dc % 2) + 16 * ex:
                                         256 * (dc % 2) + 16 * (ex + 1)],
                            w2sl[a][:, 128 * dc:128 * (dc + 1)],
                            hT[:, 16 * a:16 * a + 16], start=st, stop=sp), None))
                    ents.append((lambda e, st, sp, dc=dc, ex=ex: e.matmul(
                        pso[dc // 2][:, 256 * (dc % 2) + 16 * ex:
                                     256 * (dc % 2) + 16 * (ex + 1)],
                        b2row[0:1, 128 * dc:128 * (dc + 1)], c.ones_row[0:1, 0:16],
                        start=st, stop=sp), None))
                    p.mmgroup(ents)
                p.bar(("tensor", "vector", "scalar"))
            p.barall()
            for dc in range(DC):
                p.v(lambda e, dc=dc: e.tensor_copy(
                    soT[dc][:], pso[dc // 2][:, 256 * (dc % 2):256 * (dc % 2) + 256]))
            p.bar(("tensor", "vector"))
            for dc in range(DC):
                for a in range(2):
                    tr = p.t(lambda e, dc=dc, a=a: e.transpose(
                        ptxx[:], soT[dc][:, 128 * a:128 * (a + 1)], c.ident[:]))
                    p.v(lambda e, dc=dc, a=a: e.tensor_copy(
                        SO[a][:, 128 * dc:128 * (dc + 1)], ptxx[:]), wait=[tr])
                    p.bar(("tensor", "vector"))
        p.barall()
        with scope(c) as oses:
            pps = [oses.enter_context(_pst(nc, f"sop{i}", [128, 512], F32))
                   for i in range(2)]
            done = {}
            for j in range(DC * 4):
                dc, nb = j // 4, j % 4
                b = j % 2
                sl = slice(512 * nb, 512 * (nb + 1))
                ents = []
                for a in range(2):
                    ents.append((lambda e, st, sp, a=a, dc=dc, sl=sl, b=b: e.matmul(
                        pps[b][:], SO[a][:, 128 * dc:128 * (dc + 1)], combT[a][:, sl],
                        start=st, stop=sp), [done.get(j - 2)] if a == 0 else None))
                last = p.mmgroup(ents)
                done[j] = p.v(lambda e, dc=dc, sl=sl, b=b: e.tensor_add(
                    c.xT[dc][:, sl], c.xT[dc][:, sl], pps[b][:]), wait=[last])
        p.barall()


L0A_CFG = dict(nh=16, scale=1.0 / 32.0, causal=False,
               wq="l0a_wqkvT", wk="l0a_wqkvT", wv="l0a_wqkvT",
               qoff=0, koff=1024, voff=2048,
               wo="l0a_woutT", bout_row="l0a_bout_row", add_y=True)
L1B_CFG = dict(nh=16, scale=1.0 / 32.0, causal=True,
               wq="l1b_wqT", wk="l1b_wkT", wv="l1b_wvT",
               qoff=0, koff=0, voff=0,
               bq_col="l1b_bq_col", bk_col="l1b_bk_col", bv_row="l1b_bv_row",
               wo="l1b_woT", bout_row="l1b_bo_row", add_y=False)


def emit_step(c, stp):
    if stp == "l0a":
        emit_ln2(c, c.xT, c.yT)
        emit_attn_std(c, L0A_CFG)
    elif stp == "moe":
        emit_ln2(c, c.xT, c.yT)
        emit_moe(c)
    elif stp == "l0b":
        emit_ln2(c, c.xT, c.yT)
        emit_l0b(c)
    elif stp == "l1a":
        emit_ln2(c, c.xT, c.yT)
        emit_l1a(c)
    elif stp == "smoe":
        emit_ln2(c, c.xT, c.yT)
        emit_smoe(c)
    elif stp == "l1b":
        emit_ln2(c, c.xT, c.yT)
        emit_attn_std(c, L1B_CFG)
    elif stp == "lntest":
        emit_ln2(c, c.xT, c.yT)
        for i in range(DC):
            c.p.v(lambda e, i=i: e.tensor_copy(c.xT[i][:], c.yT[i][:]))
        c.p.barall()
        c.p.seal()
    else:
        raise ValueError(stp)


def build_program(W, steps):
    nc = bass.Bass("TRN2", target_bir_lowering=False)
    with ExitStack() as es:
        p = P(nc, es)
        c = make_ctx(nc, es, p, W, [("xin", [D, N], "ExternalInput"),
                                    ("xout", [D, N], "ExternalOutput")])
        p.blk = es.enter_context(nc.Block())
        load_consts(c)
        tok = None
        for i in range(DC):
            tok = p.dma(lambda e, i=i: e.dma_start(
                out=c.xT[i][:], in_=c.hbm["xin"][128 * i:128 * (i + 1), :]))
        p.dwait(tok)
        p.barall()
        p.seal()
        for stp in steps:
            emit_step(c, stp)
        tok = None
        for i in range(DC):
            tok = p.dma(lambda e, i=i: e.dma_start(
                out=c.hbm["xout"][128 * i:128 * (i + 1), :], in_=c.xT[i][:]),
                wait=[("dve", p.cnt.get("dve", 0)), ("act", p.cnt.get("act", 0))])
        p.dwait(tok)
        p.seal()
    return nc


def kernel(**inputs):
    _ = (inputs["L0_attn_b_v_lod1"], inputs["L0_attn_b_k_lod2"],
         inputs["L0_attn_b_v_lod2"])
    W = prep_weights(inputs)
    x = np.ascontiguousarray(np.asarray(inputs["L0_attn_a_x"], dtype=np.float32))
    nc = build_program(W, list(ALL_STEPS))
    in_maps = []
    for ci in range(8):
        m = dict(W)
        m["xin"] = np.ascontiguousarray(x[ci].T)
        in_maps.append(m)
    res = run_bass_kernel_spmd(nc, in_maps, core_ids=list(range(8)))
    return np.ascontiguousarray(
        np.stack([np.asarray(res.results[ci]["xout"]).T for ci in range(8)],
                 0)).astype(np.float32)
